# version 01
# baseline (speedup 1.0000x reference)
import contextlib

import numpy as np
import ml_dtypes

import concourse.bass as bass
import concourse.mybir as mybir
from concourse.alu_op_type import AluOpType
from concourse.bass_utils import run_bass_kernel_spmd

F32 = mybir.dt.float32
BF16 = mybir.dt.bfloat16
I32 = mybir.dt.int32
AF = mybir.ActivationFunctionType
bf16 = ml_dtypes.bfloat16

B, T, D = 4, 4096, 1024
NKV = 32
NG = 4
INNER = 4096
LN127 = float(np.log(127.0))
LN8 = float(np.log(8.0))
CONSTS = (0.0, 1e-5, -LN8, LN127)

DEV = {"taps": False, "do_p0": True, "do_p1": True, "do_p2": True, "do_p3": True, "ng": NG, "nkv": NKV, "halves": 2, "do_bounds": True, "do_norms": True, "do_rope": True, "do_tr": True, "p2_half": -1, "h1_kv": 32, "h1_wload": 1, "h1_bounds": 1}


class Prog:
    ENGINES = ("sync", "tensor", "scalar", "vector", "gpsimd")

    def __init__(self, nc, ctx):
        self.nc = nc
        self.ops = {e: [] for e in self.ENGINES}
        self.sems = {}
        self.last_wait = {e: {} for e in self.ENGINES}
        for name in ("L", "S", "P", "A", "V", "G"):
            h = ctx.enter_context(nc.semaphore(name))
            self.sems[name] = [h, 0]

    def val(self, s):
        return self.sems[s][1]

    def wait(self, eng, s, v=None):
        if v is None:
            v = self.sems[s][1]
        if v is None or v <= 0:
            return
        if self.last_wait[eng].get(s, -1) >= v:
            return
        self.last_wait[eng][s] = v
        h = self.sems[s][0]
        self.ops[eng].append(lambda e, h=h, v=v: e.wait_ge(h, v))

    def op(self, eng, fn, inc=None, k=1):
        if inc is None:
            self.ops[eng].append(fn)
            return None
        h = self.sems[inc][0]
        self.sems[inc][1] += k
        self.ops[eng].append(lambda e, fn=fn, h=h, k=k: fn(e).then_inc(h, k))
        return self.sems[inc][1]

    def mark(self, eng, sem, k=1):
        h = self.sems[sem][0]
        self.sems[sem][1] += k
        prev = self.ops[eng].pop()
        self.ops[eng].append(lambda e, p=prev, h=h, k=k: p(e).then_inc(h, k))
        return self.sems[sem][1]


def kv_slot_map(role):
    if role == 0:
        return list(range(32))
    m = [-1] * 32
    for k in range(4):
        for j in range(4):
            m[8 * k + 4 + j] = 8 * k + j
    for k in range(1, 4):
        for j in range(4):
            m[8 * k + j] = 8 * k - 4 + j
    return m


def q_rows_tiles(role):
    m = kv_slot_map(role)
    return [m[8 * G + 4 + i] for G in range(4) for i in range(4)]


def build_program(dev=None):
    dv = dict(DEV)
    if dev:
        dv.update(dev)
    ng = dv["ng"]
    nkv = dv["nkv"]

    nc = bass.Bass("TRN2", target_bir_lowering=False)
    dt = nc.dram_tensor
    io = {}
    io["xT"] = dt("xT", [D, NKV * 128], F32, kind="ExternalInput").ap()
    io["ropeC"] = dt("ropeC", [NKV * 128, 512], F32, kind="ExternalInput").ap()
    io["ropeS"] = dt("ropeS", [NKV * 128, 512], F32, kind="ExternalInput").ap()
    for n, shp in (("wqT", [D, D]), ("wkT", [D, D]), ("wvT", [D, D]),
                   ("woT", [D, D]), ("wgTf", [D, INNER]), ("wuTf", [D, INNER]),
                   ("wdTf", [INNER, D]), ("xh", [2048, D]),
                   ("bqr", [1, D]), ("bkr", [1, D]), ("bvr", [1, D]), ("bor", [1, D]),
                   ("identf", [128, 128]), ("onesrow", [1, 128]), ("onescol", [128, 1]),
                   ("scph", [128, 1]), ("phA", [128, 1]), ("phB", [128, 1]),
                   ("cst", [128, 4])):
        io[n] = dt(n, shp, F32, kind="ExternalInput").ap()
    io["identb"] = dt("identb", [128, 128], BF16, kind="ExternalInput").ap()

    io["y"] = dt("y", [2048, D], F32, kind="ExternalOutput").ap()
    kind_sc = "ExternalOutput" if dv["taps"] else "Internal"
    io["attn_sc"] = dt("attn_sc", [2048, D], BF16, kind=kind_sc).ap()
    if dv["taps"]:
        io["tap_qt"] = dt("tap_qt", [128, 4, 2048], BF16, kind="ExternalOutput").ap()
        io["tap_kt"] = dt("tap_kt", [128, 4, NKV * 128], BF16, kind="ExternalOutput").ap()
        io["tap_vs"] = dt("tap_vs", [128, NKV, 8, 65], BF16, kind="ExternalOutput").ap()
        io["tap_nm"] = dt("tap_nm", [128, 8], F32, kind="ExternalOutput").ap()
        io["tap_sc2"] = dt("tap_sc2", [2048, D], BF16, kind="ExternalOutput").ap()
    kind_w = "ExternalOutput" if dv["taps"] else "Internal"
    io["wgq_sc"] = dt("wgq_sc", [D, INNER], BF16, kind=kind_w).ap()
    io["wuq_sc"] = dt("wuq_sc", [D, INNER], BF16, kind=kind_w).ap()
    io["wdq_sc"] = dt("wdq_sc", [INNER, D], BF16, kind=kind_w).ap()
    if dv["taps"]:
        io["tap_h1"] = dt("tap_h1", [2048, D], F32, kind="ExternalOutput").ap()
        io["tap_gu"] = dt("tap_gu", [2048, INNER], F32, kind="ExternalOutput").ap()
        io["tap_p0"] = dt("tap_p0", [128, 40], F32, kind="ExternalOutput").ap()

    with contextlib.ExitStack() as ctx:
        pr = Prog(nc, ctx)
        _body(nc, pr, ctx, io, dv)
        block = ctx.enter_context(nc.Block())

        @block.sync
        def _(e):
            for fn in pr.ops["sync"]:
                fn(e)

        @block.tensor
        def _(e):
            for fn in pr.ops["tensor"]:
                fn(e)

        @block.scalar
        def _(e):
            for fn in pr.ops["scalar"]:
                fn(e)

        @block.vector
        def _(e):
            for fn in pr.ops["vector"]:
                fn(e)

        @block.gpsimd
        def _(e):
            for fn in pr.ops["gpsimd"]:
                fn(e)

    return nc


def _body(nc, pr, ctx, io, dv):
    ng, nkv = dv["ng"], dv["nkv"]
    sb = lambda name, shape, dtype: ctx.enter_context(nc.sbuf_tensor(name, shape, dtype))

    def dma(dst, src):
        return pr.op("sync", lambda e, d=dst, s=src: e.dma_start(out=d, in_=s), inc="L", k=16)

    def store(dst, src):
        return pr.op("sync", lambda e, d=dst, s=src: e.dma_start(
            out=d, in_=s, allow_slow_non_contiguous=True), inc="S", k=16)

    def vbounce():
        pr.wait("gpsimd", "V")
        gb = pr.op("gpsimd", lambda e: e.engine_nop(), inc="G")
        pr.wait("vector", "G", gb)

    s_idf = sb("s_idf", [128, 128], F32)
    s_idb = sb("s_idb", [128, 128], BF16)
    s_ones = sb("s_ones", [1, 128], F32)
    s_onesc = sb("s_onesc", [128, 1], F32)
    s_bq = sb("s_bq", [1, D], F32)
    s_bk = sb("s_bk", [1, D], F32)
    s_bv = sb("s_bv", [1, D], F32)
    s_bo = sb("s_bo", [1, D], F32)
    s_scph = sb("s_scph", [128, 1], F32)
    s_phA = sb("s_phA", [128, 1], F32)
    s_phB = sb("s_phB", [128, 1], F32)
    s_wm = sb("s_wm", [128, 6], F32)
    s_negM = sb("s_negM", [128, 4], F32)
    s_negM2 = sb("s_negM2", [128, 4], F32)
    s_cst = sb("s_cst", [128, 4], F32)

    for n, d_ in (("identf", s_idf), ("identb", s_idb), ("onesrow", s_ones),
                  ("onescol", s_onesc), ("bqr", s_bq), ("bkr", s_bk),
                  ("bvr", s_bv), ("bor", s_bo), ("scph", s_scph),
                  ("phA", s_phA), ("phB", s_phB), ("cst", s_cst)):
        lcst = dma(d_[:], io[n][:])
    for ci, cv in enumerate(CONSTS):
        nc.const_aps.register(F32, cv, s_cst[:, ci:ci + 1])
    for eng in ("tensor", "scalar", "vector", "gpsimd"):
        pr.wait(eng, "L", lcst)

    if not dv["do_p0"]:
        pr.wait("sync", "S")
        if not dv["do_p1"]:
            return
    if dv["do_p0"]:
     with contextlib.ExitStack() as c0:
        e0 = c0.enter_context
        w_ld = [e0(nc.sbuf_tensor(f"w_ld{i}", [128, 4096], F32)) for i in range(2)]
        w_i32 = e0(nc.sbuf_tensor("w_i32", [128, 4096], I32))
        w_q = [e0(nc.sbuf_tensor(f"w_q{i}", [128, 4096], BF16)) for i in range(2)]
        sums = e0(nc.sbuf_tensor("sums", [128, 32], F32))
        rsum = e0(nc.sbuf_tensor("rsum", [128, 1], F32))
        msc = e0(nc.sbuf_tensor("msc", [1, 4], F32))
        ps_m = e0(nc.psum_tensor("ps_m", [128, 2], F32))

        for wi, (wf, wsc, cols, nchunk) in enumerate((
                (io["wgTf"], io["wgq_sc"], 4096, 8),
                (io["wuTf"], io["wuq_sc"], 4096, 8),
                (io["wdTf"], io["wdq_sc"], 1024, 32))):
            nelem = float(D * INNER)
            red_marks = {}
            for ci in range(nchunk):
                buf = w_ld[ci % 2]
                pr.wait("sync", "V", red_marks.get(ci - 2))
                lm = dma(buf[:, 0:cols], wf[ci * 128:(ci + 1) * 128, :])
                pr.wait("vector", "L", lm)
                red_marks[ci] = pr.op("vector", lambda e, b=buf, c=cols, ci=ci: e.reduce_sum(
                    out=sums[:, ci:ci + 1], in_=b[:, 0:c],
                    axis=mybir.AxisListType.X, apply_absolute_value=True), inc="V")
            bounce = pr.val("V")
            pr.wait("gpsimd", "V", bounce)
            gb = pr.op("gpsimd", lambda e: e.engine_nop(), inc="G")
            pr.wait("vector", "G", gb)
            vm = pr.op("vector", lambda e, n=nchunk: e.reduce_sum(
                out=rsum[:], in_=sums[:, 0:n], axis=mybir.AxisListType.X), inc="V")
            pr.wait("tensor", "V", vm)
            pm = pr.op("tensor", lambda e: e.matmul(
                out=ps_m[0:1, 0:1], lhsT=s_onesc[:], rhs=rsum[:],
                start=True, stop=True), inc="P")
            pr.wait("vector", "P", pm)
            vm = pr.op("vector", lambda e, n=nelem: e.tensor_scalar(
                out=msc[0:1, 0:1], in0=ps_m[0:1, 0:1], scalar1=1.0 / n,
                scalar2=1e-5, op0=AluOpType.mult, op1=AluOpType.max), inc="V")
            pr.wait("scalar", "V", vm)
            am = pr.op("scalar", lambda e: e.activation(
                out=msc[0:1, 1:2], in_=msc[0:1, 0:1], func=AF.Ln), inc="A")
            pr.wait("vector", "A", am)
            vm = pr.op("vector", lambda e: e.tensor_copy(
                out=msc[0:1, 2:3], in_=msc[0:1, 1:2]), inc="V")
            pr.wait("scalar", "V", vm)
            am = pr.op("scalar", lambda e: e.activation(
                out=msc[0:1, 3:4], in_=msc[0:1, 2:3], func=AF.Exp, scale=-1.0), inc="A")
            pr.wait("tensor", "A", am)
            pm = pr.op("tensor", lambda e: e.matmul(
                out=ps_m[:, 0:1], lhsT=s_ones[:], rhs=msc[0:1, 3:4],
                start=True, stop=True), inc="P")
            pr.wait("vector", "P", pm)
            pr.op("vector", lambda e, w=wi: e.tensor_copy(
                out=s_wm[:, 2 * w:2 * w + 1], in_=ps_m[:, 0:1]), inc="V")
            vm = pr.val("V")
            pr.wait("tensor", "V", vm)
            pm = pr.op("tensor", lambda e: e.matmul(
                out=ps_m[:, 1:2], lhsT=s_ones[:], rhs=msc[0:1, 0:1],
                start=True, stop=True), inc="P")
            pr.wait("vector", "P", pm)
            vm = pr.op("vector", lambda e, w=wi: e.tensor_copy(
                out=s_wm[:, 2 * w + 1:2 * w + 2], in_=ps_m[:, 1:2]), inc="V")

            if dv["taps"] and wi == 0:
                pr.wait("sync", "V")
                pr.wait("sync", "A")
                store(io["tap_p0"][:, 0:32], sums[:])
                store(io["tap_p0"][:, 32:33], rsum[:])
                store(io["tap_p0"][0:1, 33:37], msc[:])
                store(io["tap_p0"][:, 37:38], s_wm[:, 0:1])
            st_marks = {}
            q_marks = {}
            for ci in range(nchunk):
                buf = w_ld[ci % 2]
                qb = w_q[ci % 2]
                pr.wait("sync", "V", q_marks.get(ci - 2))
                lm = dma(buf[:, 0:cols], wf[ci * 128:(ci + 1) * 128, :])
                pr.wait("vector", "L", lm)
                pr.wait("vector", "S", st_marks.get(ci - 2))
                pr.op("vector", lambda e, b=buf, c=cols, w=wi: e.tensor_scalar(
                    out=b[:, 0:c], in0=b[:, 0:c], scalar1=s_wm[:, 2 * w:2 * w + 1],
                    scalar2=None, op0=AluOpType.mult), inc="V")
                pr.op("vector", lambda e, b=buf, c=cols: e.tensor_copy(
                    out=w_i32[:, 0:c], in_=b[:, 0:c]), inc="V")
                vm = pr.op("vector", lambda e, q=qb, c=cols: e.tensor_scalar(
                    out=q[:, 0:c], in0=w_i32[:, 0:c], scalar1=1.0, scalar2=-1.0,
                    op0=AluOpType.min, op1=AluOpType.max), inc="V")
                q_marks[ci] = vm
                pr.wait("sync", "V", vm)
                st_marks[ci] = store(wsc[ci * 128:(ci + 1) * 128, :], qb[:, 0:cols])

    for eng in ("sync", "tensor", "scalar", "vector", "gpsimd"):
        pr.wait(eng, "S")

    if not dv["do_p1"]:
        pr.wait("sync", "S")
        return

    for half in range(dv["halves"]):
        hoff = half * 512
        with contextlib.ExitStack() as c1:
            e1 = c1.enter_context
            QT = e1(nc.sbuf_tensor(f"QT{half}", [128, 4, ng * 512], BF16))
            KT = e1(nc.sbuf_tensor(f"KT{half}", [128, 4, nkv * 128], BF16))
            Vsb = e1(nc.sbuf_tensor(f"Vsb{half}", [128, nkv, 8, 65], BF16))
            outn = e1(nc.sbuf_tensor(f"outn{half}", [128, 4 * ng, 512], BF16))
            racc = e1(nc.sbuf_tensor(f"racc{half}", [128, 8], F32))
            nbuf = e1(nc.sbuf_tensor(f"nbuf{half}", [128, 4, 48], F32))
            bscr = e1(nc.sbuf_tensor(f"bscr{half}", [4, 8], F32))
            brow = e1(nc.sbuf_tensor(f"brow{half}", [1, 4], F32))

            with contextlib.ExitStack() as c2:
                e2 = c2.enter_context
                wkv = e2(nc.sbuf_tensor(f"wkv{half}", [128, 8, 1024], BF16))
                wqb = e2(nc.sbuf_tensor(f"wqb{half}", [128, 8, 512], BF16))
                wldf = e2(nc.sbuf_tensor(f"wldf{half}", [128, 512], F32))
                xtl = [e2(nc.sbuf_tensor(f"xtl{half}_{i}", [128, 8, 128], F32)) for i in range(2)]
                xtb = [e2(nc.sbuf_tensor(f"xtb{half}_{i}", [128, 8, 128], BF16)) for i in range(2)]
                rC = [e2(nc.sbuf_tensor(f"rC{half}_{i}", [128, 512], F32)) for i in range(2)]
                rS = [e2(nc.sbuf_tensor(f"rS{half}_{i}", [128, 512], F32)) for i in range(2)]
                ktok = e2(nc.sbuf_tensor(f"ktok{half}", [128, 512], F32))
                qtok = e2(nc.sbuf_tensor(f"qtok{half}", [128, 512], F32))
                m1 = e2(nc.sbuf_tensor(f"m1{half}", [128, 512], F32))
                m2 = e2(nc.sbuf_tensor(f"m2{half}", [128, 512], F32))
                krope = e2(nc.sbuf_tensor(f"krope{half}", [128, 512], BF16))
                qrope = e2(nc.sbuf_tensor(f"qrope{half}", [128, 512], BF16))
                sqs = e2(nc.sbuf_tensor(f"sqs{half}", [128, 512], F32))
                red4 = e2(nc.sbuf_tensor(f"red4{half}", [128, 4], F32))
                ps_k = e2(nc.psum_tensor(f"ps_k{half}", [128, 512], F32))
                ps_v = e2(nc.psum_tensor(f"ps_v{half}", [128, 512], F32))
                ps_q = e2(nc.psum_tensor(f"ps_q{half}", [128, 512], F32))
                ps_trb = e2(nc.psum_tensor(f"ps_trb{half}", [128, 512], BF16))
                ps_tr4 = e2(nc.psum_tensor(f"ps_tr4{half}", [128, 256], F32))

                wload_list = (((io["wkT"], 0), (io["wvT"], 1), (io["wqT"], 2))
                              if (half == 0 or dv["h1_wload"]) else ())
                for wi, (wsrc, wdst) in enumerate(wload_list):
                    for i in range(8):
                        pr.wait("sync", "V")
                        lm = dma(wldf[:], wsrc[i * 128:(i + 1) * 128, hoff:hoff + 512])
                        pr.wait("vector", "L", lm)
                        if wdst == 0:
                            pr.op("vector", lambda e, i=i: e.tensor_copy(
                                out=wkv[:, i, 0:512], in_=wldf[:]), inc="V")
                        elif wdst == 1:
                            pr.op("vector", lambda e, i=i: e.tensor_copy(
                                out=wkv[:, i, 512:1024], in_=wldf[:]), inc="V")
                        else:
                            pr.op("vector", lambda e, i=i: e.tensor_copy(
                                out=wqb[:, i, :], in_=wldf[:]), inc="V")

                gm_ones = pr.op("gpsimd", lambda e: e.memset(Vsb[:], 1.0), inc="G")
                pr.op("vector", lambda e: e.memset(racc[:], 0.0))
                pr.op("vector", lambda e: e.memset(nbuf[:], 0.0), inc="V")

                mk = {}
                nkv_h = nkv if half == 0 else min(nkv, dv["h1_kv"])
                for s in range(nkv_h):
                    bi = s % 2
                    is_q = (s % 8) >= 4
                    G = s // 8
                    qcol = G * 512 + ((s % 8) - 4) * 128

                    pr.wait("sync", "V", mk.get(("cast", s - 2)))
                    lx = dma(xtl[bi][:], io["xT"][:, s * 128:(s + 1) * 128]
                             .rearrange("(c p) t -> p c t", p=128))
                    pr.wait("sync", "G", mk.get(("rope", s - 2)))
                    dma(rC[bi][:], io["ropeC"][s * 128:(s + 1) * 128, :])
                    ls = dma(rS[bi][:], io["ropeS"][s * 128:(s + 1) * 128, :])
                    pr.wait("vector", "L", lx)
                    pr.wait("vector", "P", mk.get(("pm", s - 2)))
                    mk[("cast", s)] = pr.op("vector", lambda e, b=bi: e.tensor_copy(
                        out=xtb[b][:], in_=xtl[b][:]), inc="V")

                    pr.wait("tensor", "V", mk[("cast", s)])
                    pr.wait("tensor", "V", mk.get(("pevac", s - 1)))
                    pr.op("tensor", lambda e, ho=hoff: e.matmul(
                        out=ps_k[:], lhsT=s_ones[:], rhs=s_bk[0:1, ho:ho + 512],
                        start=True, stop=False))
                    pr.op("tensor", lambda e, ho=hoff: e.matmul(
                        out=ps_v[:], lhsT=s_ones[:], rhs=s_bv[0:1, ho:ho + 512],
                        start=True, stop=False))
                    for i in range(8):
                        pr.op("tensor", lambda e, b=bi, i=i: e.matmul(
                            out=ps_k[:], lhsT=xtb[b][:, i, :], rhs=wkv[:, i, 0:512],
                            start=False, stop=(i == 7)))
                        pr.op("tensor", lambda e, b=bi, i=i: e.matmul(
                            out=ps_v[:], lhsT=xtb[b][:, i, :], rhs=wkv[:, i, 512:1024],
                            start=False, stop=(i == 7)))
                    if is_q:
                        pr.op("tensor", lambda e, ho=hoff: e.matmul(
                            out=ps_q[:], lhsT=s_ones[:], rhs=s_bq[0:1, ho:ho + 512],
                            start=True, stop=False))
                        for i in range(8):
                            pr.op("tensor", lambda e, b=bi, i=i: e.matmul(
                                out=ps_q[:], lhsT=xtb[b][:, i, :], rhs=wqb[:, i, :],
                                start=False, stop=(i == 7)))
                    mk[("pm", s)] = pr.mark("tensor", "P")

                    pr.wait("vector", "P", mk[("pm", s)])
                    if s == 0:
                        pr.wait("vector", "G", gm_ones)
                    pr.wait("vector", "G", mk.get(("rope", s - 1)))
                    pr.op("vector", lambda e: e.tensor_copy(out=ktok[:], in_=ps_k[:]), inc="V")
                    pr.op("vector", lambda e, s=s: e.tensor_copy(
                        out=Vsb[:, s, :, 0:64],
                        in_=ps_v.ap().rearrange("p (h d) -> p h d", d=64)), inc="V")
                    if is_q:
                        pr.op("vector", lambda e: e.tensor_copy(out=qtok[:], in_=ps_q[:]), inc="V")
                    mk[("pevac", s)] = pr.val("V")

                    pr.wait("gpsimd", "V", mk[("pevac", s)])
                    pr.wait("gpsimd", "L", ls)
                    pr.wait("gpsimd", "P", mk.get(("tpm", s - 1)))
                    pr.wait("gpsimd", "A", mk.get(("sq", s - 1)))

                    def rope(tin, tout, b=bi):
                        pr.op("gpsimd", lambda e, t=tin, b=b: e.tensor_tensor(
                            out=m1[:], in0=t[:], in1=rC[b][:], op=AluOpType.mult))
                        pr.op("gpsimd", lambda e, t=tin, b=b: e.tensor_tensor(
                            out=m2[:, 0::2], in0=t[:, 1::2], in1=rS[b][:, 0::2], op=AluOpType.mult))
                        pr.op("gpsimd", lambda e, t=tin, b=b: e.tensor_tensor(
                            out=m2[:, 1::2], in0=t[:, 0::2], in1=rS[b][:, 1::2], op=AluOpType.mult))
                        return pr.op("gpsimd", lambda e, o=tout: e.tensor_tensor(
                            out=o[:], in0=m1[:], in1=m2[:], op=AluOpType.add), inc="G")

                    gk = rope(ktok, krope)
                    gq = rope(qtok, qrope) if is_q else gk
                    mk[("rope", s)] = gq

                    if not dv["do_norms"]:
                        mk[("sq", s)] = 0
                        mk[("red", s)] = 0
                    else:
                        pr.wait("scalar", "G", gk)
                        pr.wait("scalar", "V", mk.get(("red", s - 1)))
                        amk = pr.op("scalar", lambda e: e.activation(
                            out=sqs[:], in_=krope[:], func=AF.Square), inc="A")
                        pr.wait("vector", "A", amk)
                        vr = pr.op("vector", lambda e, s=s: e.tensor_reduce(
                            out=nbuf[:, :, s:s + 1],
                            in_=sqs.ap().rearrange("p (h d) -> p h d", d=128),
                            axis=mybir.AxisListType.X, op=AluOpType.add), inc="V")
                        if is_q:
                            qi = 32 + (s // 8) * 4 + (s % 8 - 4)
                            pr.wait("scalar", "V", vr)
                            pr.wait("scalar", "G", mk[("rope", s)])
                            amq = pr.op("scalar", lambda e: e.activation(
                                out=sqs[:], in_=qrope[:], func=AF.Square), inc="A")
                            pr.wait("vector", "A", amq)
                            vr = pr.op("vector", lambda e, qi=qi: e.tensor_reduce(
                                out=nbuf[:, :, qi:qi + 1],
                                in_=sqs.ap().rearrange("p (h d) -> p h d", d=128),
                                axis=mybir.AxisListType.X, op=AluOpType.add), inc="V")
                            mk[("sq", s)] = amq
                        else:
                            mk[("sq", s)] = amk
                        mk[("red", s)] = vr

                    pr.wait("tensor", "G", mk[("rope", s)])
                    pr.wait("tensor", "V", mk.get(("tevac", s - 1)))
                    for j in range(4):
                        pr.op("tensor", lambda e, j=j: e.transpose(
                            out=ps_trb[:, j * 128:(j + 1) * 128],
                            in_=krope[:, j * 128:(j + 1) * 128], identity=s_idb[:]), inc="P")
                    pm = pr.val("P")
                    pr.wait("vector", "P", pm)
                    pr.op("vector", lambda e, s=s: e.tensor_copy(
                        out=KT[:, :, s * 128:(s + 1) * 128],
                        in_=ps_trb.ap().rearrange("p (h t) -> p h t", t=128)), inc="V")
                    tm = pr.val("V")
                    if is_q:
                        pr.wait("tensor", "V", tm)
                        for j in range(4):
                            pr.op("tensor", lambda e, j=j: e.transpose(
                                out=ps_trb[:, j * 128:(j + 1) * 128],
                                in_=qrope[:, j * 128:(j + 1) * 128], identity=s_idb[:]), inc="P")
                        pm = pr.val("P")
                        pr.wait("vector", "P", pm)
                        pr.op("vector", lambda e, q=qcol: e.tensor_copy(
                            out=QT[:, :, q:q + 128],
                            in_=ps_trb.ap().rearrange("p (h t) -> p h t", t=128)), inc="V")
                        tm = pr.val("V")
                    mk[("tevac", s)] = tm
                    mk[("tpm", s)] = pm

                if not dv["do_bounds"] or (half == 1 and not dv["h1_bounds"]):
                    continue
                vbounce()
                pr.op("vector", lambda e: e.tensor_reduce(
                    out=racc.ap().rearrange("p (a b) -> p a b", b=1)[:, 0:4, :],
                    in_=nbuf[:, :, 32:48], axis=mybir.AxisListType.X,
                    op=AluOpType.max))
                pr.op("vector", lambda e: e.tensor_reduce(
                    out=racc.ap().rearrange("p (a b) -> p a b", b=1)[:, 4:8, :],
                    in_=nbuf[:, :, 0:32], axis=mybir.AxisListType.X,
                    op=AluOpType.max), inc="V")
                vbounce()
                pr.op("vector", lambda e: e.tensor_scalar(
                    out=racc[:], in0=racc[:], scalar1=1e-30, scalar2=None,
                    op0=AluOpType.max), inc="V")
                pr.wait("tensor", "V")
                pr.op("tensor", lambda e: e.transpose(
                    out=ps_tr4[0:4, 0:128], in_=racc[:, 0:4], identity=s_idf[:]), inc="P")
                pm = pr.op("tensor", lambda e: e.transpose(
                    out=ps_tr4[0:4, 128:256], in_=racc[:, 4:8], identity=s_idf[:]), inc="P")
                pr.wait("vector", "P", pm)
                pr.op("vector", lambda e: e.tensor_reduce(
                    out=bscr[0:4, 0:1], in_=ps_tr4[0:4, 0:128],
                    axis=mybir.AxisListType.X, op=AluOpType.max), inc="V")
                vm = pr.op("vector", lambda e: e.tensor_reduce(
                    out=bscr[0:4, 1:2], in_=ps_tr4[0:4, 128:256],
                    axis=mybir.AxisListType.X, op=AluOpType.max), inc="V")
                pr.wait("scalar", "V", vm)
                am = pr.op("scalar", lambda e: e.activation(
                    out=bscr[0:4, 2:3], in_=bscr[0:4, 0:1], func=AF.Ln), inc="A")
                pr.wait("vector", "A", am)
                vm = pr.op("vector", lambda e: e.tensor_copy(
                    out=bscr[0:4, 4:5], in_=bscr[0:4, 2:3]), inc="V")
                pr.wait("scalar", "V", vm)
                am = pr.op("scalar", lambda e: e.activation(
                    out=bscr[0:4, 3:4], in_=bscr[0:4, 1:2], func=AF.Ln), inc="A")
                pr.wait("vector", "A", am)
                vm = pr.op("vector", lambda e: e.tensor_tensor(
                    out=bscr[0:4, 5:6], in0=bscr[0:4, 4:5], in1=bscr[0:4, 3:4],
                    op=AluOpType.add), inc="V")
                pr.wait("scalar", "V", vm)
                am = pr.op("scalar", lambda e: e.activation(
                    out=bscr[0:4, 6:7], in_=bscr[0:4, 5:6], func=AF.Exp,
                    scale=0.5, bias=-LN8), inc="A")
                pr.wait("vector", "A", am)
                vm = pr.op("vector", lambda e: e.tensor_scalar(
                    out=bscr[0:4, 7:8], in0=bscr[0:4, 6:7], scalar1=-1.0,
                    scalar2=None, op0=AluOpType.mult), inc="V")
                pr.wait("tensor", "V", vm)
                pm = pr.op("tensor", lambda e: e.transpose(
                    out=ps_tr4[0:1, 0:4], in_=bscr[0:4, 7:8],
                    identity=s_idf[0:4, 0:4]), inc="P")
                pr.wait("vector", "P", pm)
                vm = pr.op("vector", lambda e: e.tensor_copy(
                    out=brow[0:1, 0:4], in_=ps_tr4[0:1, 0:4]), inc="V")
                pr.wait("tensor", "V", vm)
                pm = pr.op("tensor", lambda e: e.matmul(
                    out=ps_tr4[:, 0:4], lhsT=s_ones[:], rhs=brow[0:1, 0:4],
                    start=True, stop=True), inc="P")
                pr.wait("vector", "P", pm)
                pr.op("vector", lambda e: e.tensor_copy(out=s_negM[:], in_=ps_tr4[:, 0:4]), inc="V")
                vm = pr.op("vector", lambda e: e.tensor_scalar(
                    out=s_negM2[:], in0=s_negM[:], scalar1=s_phA[:, 0:1],
                    scalar2=s_phB[:, 0:1], op0=AluOpType.mult, op1=AluOpType.add), inc="V")
                pr.wait("scalar", "V", vm)
                pr.wait("tensor", "V", vm)

                if dv["taps"] and half == 0:
                    pr.wait("sync", "V", vm)
                    store(io["tap_qt"][:], QT[:])
                    store(io["tap_kt"][:], KT[:])
                    store(io["tap_vs"][:], Vsb[:])
                    store(io["tap_nm"][:, 0:4], s_negM[:])
                    store(io["tap_nm"][:, 4:8], s_negM2[:])

            if not dv["do_p2"]:
                continue
            if dv["p2_half"] >= 0 and half != dv["p2_half"]:
                continue

            with contextlib.ExitStack() as c3:
                e3 = c3.enter_context
                ps_S = e3(nc.psum_tensor(f"ps_S{half}", [128, 2048], F32))
                ps_oA = e3(nc.psum_tensor(f"ps_oA{half}", [128, 512], F32))
                ps_oB = e3(nc.psum_tensor(f"ps_oB{half}", [128, 512], F32))
                ps_to = e3(nc.psum_tensor(f"ps_to{half}", [128, 260], F32))
                Pt = e3(nc.sbuf_tensor(f"Pt{half}", [128, 2048], BF16))
                o_raw = e3(nc.sbuf_tensor(f"o_raw{half}", [65, 2, 512], F32))
                tok = e3(nc.sbuf_tensor(f"tok{half}", [128, 2, 4, 65], F32))
                rcl = e3(nc.sbuf_tensor(f"rcl{half}", [128, 8], F32))
                rc = e3(nc.sbuf_tensor(f"rc{half}", [128, 8], F32))

                exp_mark = 0
                pv_mark = 0
                oraw_trans_mark = 0
                oraw_copy_prev = 0
                last_norm = 0
                for hp in range(4):
                    for G in range(ng):
                        E = 8 * (G + 1)
                        npair = E // 2
                        for p_ in range(npair):
                            j0 = 2 * p_
                            sl = [(8 * G + 7 - j) if j < 4 else (j - 4) for j in (j0, j0 + 1)]
                            pr.wait("tensor", "A", exp_mark)
                            for h in range(2):
                                for jj in range(2):
                                    off = (2 * h + jj) * 512
                                    pr.op("tensor", lambda e, h=h, s=sl[jj], off=off, hp=hp, G=G: e.matmul(
                                        out=ps_S[:, off:off + 512],
                                        lhsT=KT[64 * h:64 * h + 64, hp, s * 128:(s + 1) * 128],
                                        rhs=QT[64 * h:64 * h + 64, hp, G * 512:(G + 1) * 512],
                                        start=True, stop=True,
                                        tile_position=(64 * h, 0)))
                            pm = pr.mark("tensor", "P")
                            pr.wait("scalar", "P", pm)
                            pr.wait("scalar", "P", pv_mark)
                            if p_ in (2, 3):
                                am = pr.op("scalar", lambda e, hp=hp: e.activation(
                                    out=Pt[:], in_=ps_S[:], func=AF.Exp,
                                    bias=s_negM2[:, hp:hp + 1], scale=s_scph[:, 0:1]), inc="A")
                            else:
                                am = pr.op("scalar", lambda e, hp=hp: e.activation(
                                    out=Pt[:], in_=ps_S[:], func=AF.Exp,
                                    bias=s_negM[:, hp:hp + 1], scale=0.125), inc="A")
                            exp_mark = am
                            dep_s, dep_v = "A", am
                            if p_ < 2:
                                pr.wait("gpsimd", "A", am)
                                for jj in range(2):
                                    j = j0 + jj
                                    gm = pr.op("gpsimd", lambda e, jj=jj, base=-384 + 128 * j: e.affine_select(
                                        out=Pt.ap().rearrange("p (x c) -> p x c", c=512)[:, jj::2, :],
                                        in_=Pt.ap().rearrange("p (x c) -> p x c", c=512)[:, jj::2, :],
                                        pattern=[[0, 2], [1, 512]],
                                        compare_op=AluOpType.is_ge, fill=0.0,
                                        base=base, channel_multiplier=-1), inc="G")
                                dep_s, dep_v = "G", gm
                            pr.wait("tensor", dep_s, dep_v)
                            if p_ == 0:
                                pr.wait("tensor", "A", oraw_copy_prev)
                            first = (p_ == 0)
                            last = (p_ == npair - 1)
                            for h in range(2):
                                bank = ps_oA if h == 0 else ps_oB
                                for jj in range(2):
                                    off = (2 * h + jj) * 512
                                    hh = 2 * hp + h
                                    pr.op("tensor", lambda e, b=bank, s=sl[jj], hh=hh, off=off,
                                          st=(first and jj == 0), sp=(last and jj == 1): e.matmul(
                                        out=b[0:65, :], lhsT=Vsb[:, s, hh, 0:65],
                                        rhs=Pt[:, off:off + 512], start=st, stop=sp))
                            pv_mark = pr.mark("tensor", "P")

                        pr.wait("scalar", "P", pv_mark)
                        pr.wait("scalar", "P", oraw_trans_mark)
                        pr.wait("scalar", "V", last_norm)
                        for h in range(2):
                            bank = ps_oA if h == 0 else ps_oB
                            pr.op("scalar", lambda e, b=bank, h=h: e.copy(
                                out=o_raw[0:65, h, :], in_=b[0:65, :]))
                        am = pr.mark("scalar", "A")
                        oraw_copy = am
                        oraw_copy_prev = am
                        ev_prev = 0
                        for h in range(2):
                            pr.wait("tensor", "A", oraw_copy)
                            pr.wait("tensor", "V", ev_prev)
                            for i in range(4):
                                pr.op("tensor", lambda e, h=h, i=i: e.transpose(
                                    out=ps_to[:, i * 65:(i + 1) * 65],
                                    in_=o_raw[0:65, h, i * 128:(i + 1) * 128],
                                    identity=s_idf[0:65, 0:65]), inc="P")
                            pm = pr.val("P")
                            if h == 1:
                                oraw_trans_mark = pm
                            pr.wait("vector", "P", pm)
                            ev_prev = pr.op("vector", lambda e, h=h: e.tensor_copy(
                                out=tok[:, h, :, :],
                                in_=ps_to.ap().rearrange("p (i c) -> p i c", c=65)), inc="V")
                        pr.wait("scalar", "V", ev_prev)
                        am = pr.op("scalar", lambda e: e.activation(
                            out=rcl[:], in_=tok[:, :, :, 64:65].rearrange("p h i c -> p (h i c)"),
                            func=AF.Ln), inc="A")
                        pr.wait("vector", "A", am)
                        vm = pr.op("vector", lambda e: e.tensor_copy(out=rc[:], in_=rcl[:]), inc="V")
                        pr.wait("scalar", "V", vm)
                        am = pr.op("scalar", lambda e: e.activation(
                            out=rc[:], in_=rc[:], func=AF.Exp, scale=-1.0), inc="A")
                        pr.wait("vector", "A", am)
                        for h in range(2):
                            for i in range(4):
                                hh = 2 * hp + h
                                pr.op("vector", lambda e, h=h, i=i, hh=hh, G=G: e.tensor_scalar(
                                    out=outn[:, 4 * G + i, hh * 64:(hh + 1) * 64],
                                    in0=tok[:, h, i, 0:64],
                                    scalar1=rc[:, 4 * h + i:4 * h + i + 1], scalar2=None,
                                    op0=AluOpType.mult), inc="V")
                        last_norm = pr.val("V")
                pr.wait("sync", "V", last_norm)
                store(io["attn_sc"].rearrange("(a p) c -> p a c", p=128)[:, :, hoff:hoff + 512],
                      outn[:])
                if dv["taps"]:
                    store(io["tap_sc2"].rearrange("(a p) c -> p a c", p=128)[:, :, hoff:hoff + 512],
                          outn[:])
                for eng in ("sync", "tensor", "scalar", "vector", "gpsimd"):
                    pr.wait(eng, "S")

    if not dv["do_p3"]:
        pr.wait("sync", "S")
        return

    pr.wait("sync", "S")
    with contextlib.ExitStack() as c4:
        e4 = c4.enter_context
        woTb = e4(nc.sbuf_tensor("woTb", [128, 8, 1024], BF16))
        h1 = e4(nc.sbuf_tensor("h1", [128, 4, 1024], F32))
        h2 = h1
        xq = e4(nc.sbuf_tensor("xq", [128, 1024], BF16))
        xqT = e4(nc.sbuf_tensor("xqT", [128, 8, 512], BF16))
        guT = e4(nc.sbuf_tensor("guT", [128, 32, 512], BF16))
        fg = e4(nc.sbuf_tensor("fg", [128, 4], F32))
        fu = e4(nc.sbuf_tensor("fu", [128, 4], F32))
        fd = e4(nc.sbuf_tensor("fd", [128, 4], F32))
        lnt = e4(nc.sbuf_tensor("lnt", [128, 16], F32))
        stats = e4(nc.sbuf_tensor("stats", [128, 12], F32))
        mv = e4(nc.sbuf_tensor("mv", [128, 2], F32))
        h1n = e4(nc.sbuf_tensor("h1n", [128, 1024], F32))
        hi32 = e4(nc.sbuf_tensor("hi32", [128, 1024], I32))
        xht = e4(nc.sbuf_tensor("xht", [128, 1024], F32))
        outs = e4(nc.sbuf_tensor("outs", [128, 1024], F32))
        ps_y = e4(nc.psum_tensor("ps_y", [128, 1024], F32))
        ps_tb = e4(nc.psum_tensor("ps_tb", [128, 512], BF16))

        with nc.sbuf_tensor("wldo", [128, 1024], F32) as wldo:
            for i in range(8):
                pr.wait("sync", "V")
                lm = dma(wldo[:], io["woT"][i * 128:(i + 1) * 128, :])
                pr.wait("vector", "L", lm)
                pr.op("vector", lambda e, i=i: e.tensor_copy(out=woTb[:, i, :], in_=wldo[:]), inc="V")

        def transpose_1024(src_ap_fn, dst, dst_col, n=8):
            for g0 in range(0, n, 4):
                pr.wait("tensor", "V")
                for j in range(4):
                    pr.op("tensor", lambda e, g0=g0, j=j: e.transpose(
                        out=ps_tb[:, j * 128:(j + 1) * 128],
                        in_=src_ap_fn(g0 + j), identity=s_idb[:]), inc="P")
                pm = pr.val("P")
                pr.wait("vector", "P", pm)
                pr.op("vector", lambda e, g0=g0: e.tensor_copy(
                    out=dst[:, g0:g0 + 4, dst_col:dst_col + 128],
                    in_=ps_tb.ap().rearrange("p (a b) -> p a b", b=128)), inc="V")

        nstore = 0
        prev_store = 0
        for ck in range(4):
            cA = contextlib.ExitStack()
            aol = cA.enter_context(nc.sbuf_tensor(f"aol{ck}", [128, 1024], BF16))
            aoT = cA.enter_context(nc.sbuf_tensor(f"aoT{ck}", [128, 8, 512], BF16))
            for i in range(4):
                pr.wait("sync", "V")
                lm = dma(aol[:], io["attn_sc"][(4 * ck + i) * 128:(4 * ck + i + 1) * 128, :])
                pr.wait("tensor", "L", lm)
                transpose_1024(lambda j: aol[:, j * 128:(j + 1) * 128], aoT, i * 128)

            for i in range(4):
                pr.wait("sync", "V")
                lxh = dma(xht[:], io["xh"][(4 * ck + i) * 128:(4 * ck + i + 1) * 128, :])
                pr.wait("tensor", "V")
                pr.op("tensor", lambda e: e.matmul(
                    out=ps_y[:, 0:512], lhsT=s_ones[:], rhs=s_bo[0:1, 0:512],
                    start=True, stop=False))
                pr.op("tensor", lambda e: e.matmul(
                    out=ps_y[:, 512:1024], lhsT=s_ones[:], rhs=s_bo[0:1, 512:1024],
                    start=True, stop=False))
                for c in range(8):
                    pr.op("tensor", lambda e, c=c, i=i: e.matmul(
                        out=ps_y[:, 0:512], lhsT=aoT[:, c, i * 128:(i + 1) * 128],
                        rhs=woTb[:, c, 0:512], start=False, stop=(c == 7)))
                    pr.op("tensor", lambda e, c=c, i=i: e.matmul(
                        out=ps_y[:, 512:1024], lhsT=aoT[:, c, i * 128:(i + 1) * 128],
                        rhs=woTb[:, c, 512:1024], start=False, stop=(c == 7)))
                pm = pr.mark("tensor", "P")
                pr.wait("vector", "P", pm)
                pr.wait("vector", "L", lxh)
                vm_h1 = pr.op("vector", lambda e, i=i: e.scalar_tensor_tensor(
                    out=h1[:, i, :], in0=ps_y[:], scalar=0.0, in1=xht[:],
                    op0=AluOpType.add, op1=AluOpType.add), inc="V")
                if dv["taps"]:
                    pr.wait("sync", "V", vm_h1)
                    store(io["tap_h1"][(4 * ck + i) * 128:(4 * ck + i + 1) * 128, :],
                          h1[:, i, :])

                pr.op("vector", lambda e, i=i: e.bn_stats(out=stats[:, 0:6], in_=h1[:, i, 0:512]))
                pr.op("vector", lambda e, i=i: e.bn_stats(out=stats[:, 6:12], in_=h1[:, i, 512:1024]), inc="V")
                vbounce()
                vm = pr.op("vector", lambda e: e.bn_aggr(out=mv[:], in_=stats[:]), inc="V")
                pr.wait("scalar", "V", vm)
                am = pr.op("scalar", lambda e: e.activation(
                    out=lnt[:, 0:1], in_=mv[:, 1:2], func=AF.Ln, bias=1e-5), inc="A")
                pr.wait("vector", "A", am)
                vm = pr.op("vector", lambda e: e.tensor_copy(out=lnt[:, 1:2], in_=lnt[:, 0:1]), inc="V")
                pr.wait("scalar", "V", vm)
                am = pr.op("scalar", lambda e: e.activation(
                    out=lnt[:, 2:3], in_=lnt[:, 1:2], func=AF.Exp, scale=-0.5), inc="A")
                pr.wait("vector", "A", am)
                pr.op("vector", lambda e, i=i: e.tensor_scalar(
                    out=h1n[:], in0=h1[:, i, :], scalar1=mv[:, 0:1], scalar2=lnt[:, 2:3],
                    op0=AluOpType.subtract, op1=AluOpType.mult), inc="V")
                vq = pr.op("vector", lambda e: e.tensor_reduce(
                    out=lnt[:, 3:4], in_=h1n[:], axis=mybir.AxisListType.X,
                    op=AluOpType.max, apply_absolute_value=True), inc="V")
                pr.wait("gpsimd", "V", vq)
                gq_ = pr.op("gpsimd", lambda e: e.tensor_scalar(
                    out=lnt[:, 4:5], in0=lnt[:, 3:4], scalar1=1e-5, scalar2=None,
                    op0=AluOpType.max), inc="G")
                pr.wait("scalar", "G", gq_)
                pr.wait("vector", "G", gq_)
                am = pr.op("scalar", lambda e: e.activation(
                    out=lnt[:, 5:6], in_=lnt[:, 4:5], func=AF.Ln), inc="A")
                pr.wait("vector", "A", am)
                vm = pr.op("vector", lambda e: e.tensor_copy(out=lnt[:, 6:7], in_=lnt[:, 5:6]), inc="V")
                pr.wait("scalar", "V", vm)
                am = pr.op("scalar", lambda e: e.activation(
                    out=lnt[:, 7:8], in_=lnt[:, 6:7], func=AF.Exp, scale=-1.0,
                    bias=LN127), inc="A")
                pr.wait("vector", "A", am)
                pr.op("vector", lambda e: e.tensor_scalar(
                    out=h1n[:], in0=h1n[:], scalar1=lnt[:, 7:8], scalar2=None,
                    op0=AluOpType.mult), inc="V")
                pr.op("vector", lambda e: e.tensor_copy(out=hi32[:], in_=h1n[:]), inc="V")
                vm = pr.op("vector", lambda e: e.tensor_copy(
                    out=xq[:], in_=hi32[:, 0:1024]), inc="V")
                pr.op("vector", lambda e, i=i: e.tensor_scalar(
                    out=fg[:, i:i + 1], in0=lnt[:, 4:5], scalar1=1.0 / 127.0,
                    scalar2=s_wm[:, 1:2], op0=AluOpType.mult, op1=AluOpType.mult))
                pr.op("vector", lambda e, i=i: e.tensor_scalar(
                    out=fu[:, i:i + 1], in0=lnt[:, 4:5], scalar1=1.0 / 127.0,
                    scalar2=s_wm[:, 3:4], op0=AluOpType.mult, op1=AluOpType.mult), inc="V")
                transpose_1024(lambda j: xq[:, j * 128:(j + 1) * 128], xqT, i * 128)

            cA.close()
            with contextlib.ExitStack() as c5:
                e5 = c5.enter_context
                gu = e5(nc.sbuf_tensor(f"gu{ck}", [128, 4, 4096], F32))
                guq = e5(nc.sbuf_tensor(f"guq{ck}", [128, 4096], BF16))
                wgs = e5(nc.sbuf_tensor(f"wgs{ck}", [128, 8, 512], BF16))
                wus = e5(nc.sbuf_tensor(f"wus{ck}", [128, 8, 512], BF16))
                gsb = e5(nc.sbuf_tensor(f"gsb{ck}", [128, 512], F32))
                usb = e5(nc.sbuf_tensor(f"usb{ck}", [128, 512], F32))
                ps_g = e5(nc.psum_tensor(f"ps_g{ck}", [128, 512], F32))
                ps_u = e5(nc.psum_tensor(f"ps_u{ck}", [128, 512], F32))

                for o in range(8):
                    pr.wait("sync", "P")
                    dma(wgs[:], io["wgq_sc"][:, o * 512:(o + 1) * 512]
                        .rearrange("(c p) n -> p c n", p=128))
                    lu = dma(wus[:], io["wuq_sc"][:, o * 512:(o + 1) * 512]
                             .rearrange("(c p) n -> p c n", p=128))
                    for i in range(4):
                        pr.wait("tensor", "L", lu)
                        pr.wait("tensor", "V")
                        pr.wait("tensor", "A")
                        for c in range(8):
                            pr.op("tensor", lambda e, c=c, i=i: e.matmul(
                                out=ps_g[:], lhsT=xqT[:, c, i * 128:(i + 1) * 128],
                                rhs=wgs[:, c, :], start=(c == 0), stop=(c == 7)))
                            pr.op("tensor", lambda e, c=c, i=i: e.matmul(
                                out=ps_u[:], lhsT=xqT[:, c, i * 128:(i + 1) * 128],
                                rhs=wus[:, c, :], start=(c == 0), stop=(c == 7)))
                        pm = pr.mark("tensor", "P")
                        pr.wait("scalar", "P", pm)
                        pr.wait("scalar", "V")
                        am = pr.op("scalar", lambda e, i=i: e.activation(
                            out=gsb[:], in_=ps_g[:], func=AF.Silu,
                            scale=fg[:, i:i + 1]), inc="A")
                        pr.wait("vector", "P", pm)
                        pr.op("vector", lambda e, i=i: e.tensor_scalar(
                            out=usb[:], in0=ps_u[:], scalar1=fu[:, i:i + 1],
                            scalar2=None, op0=AluOpType.mult), inc="V")
                        pr.wait("vector", "A", am)
                        pr.op("vector", lambda e, i=i, o=o: e.tensor_tensor(
                            out=gu[:, i, o * 512:(o + 1) * 512], in0=gsb[:], in1=usb[:],
                            op=AluOpType.mult), inc="V")

                if dv["taps"]:
                    pr.wait("sync", "V")
                    for i4 in range(4):
                        store(io["tap_gu"][(4 * ck + i4) * 128:(4 * ck + i4 + 1) * 128, :],
                              gu[:, i4, :])
                for i in range(4):
                    vq = pr.op("vector", lambda e, i=i: e.tensor_reduce(
                        out=lnt[:, 8:9], in_=gu[:, i, :], axis=mybir.AxisListType.X,
                        op=AluOpType.max, apply_absolute_value=True), inc="V")
                    pr.wait("gpsimd", "V", vq)
                    gq_ = pr.op("gpsimd", lambda e: e.tensor_scalar(
                        out=lnt[:, 9:10], in0=lnt[:, 8:9], scalar1=1e-5, scalar2=None,
                        op0=AluOpType.max), inc="G")
                    pr.wait("scalar", "G", gq_)
                    pr.wait("vector", "G", gq_)
                    am = pr.op("scalar", lambda e: e.activation(
                        out=lnt[:, 10:11], in_=lnt[:, 9:10], func=AF.Ln), inc="A")
                    pr.wait("vector", "A", am)
                    vm = pr.op("vector", lambda e: e.tensor_copy(
                        out=lnt[:, 11:12], in_=lnt[:, 10:11]), inc="V")
                    pr.wait("scalar", "V", vm)
                    am = pr.op("scalar", lambda e: e.activation(
                        out=lnt[:, 12:13], in_=lnt[:, 11:12], func=AF.Exp, scale=-1.0,
                        bias=LN127), inc="A")
                    pr.wait("vector", "A", am)
                    pr.op("vector", lambda e, i=i: e.tensor_scalar(
                        out=gu[:, i, :], in0=gu[:, i, :], scalar1=lnt[:, 12:13],
                        scalar2=None, op0=AluOpType.mult), inc="V")
                    for q4 in range(4):
                        pr.op("vector", lambda e, i=i, q4=q4: e.tensor_copy(
                            out=hi32[:], in_=gu[:, i, q4 * 1024:(q4 + 1) * 1024]), inc="V")
                        vm = pr.op("vector", lambda e, q4=q4: e.tensor_copy(
                            out=guq[:, q4 * 1024:(q4 + 1) * 1024], in_=hi32[:]), inc="V")
                    pr.op("vector", lambda e, i=i: e.tensor_scalar(
                        out=fd[:, i:i + 1], in0=lnt[:, 9:10], scalar1=1.0 / 127.0,
                        scalar2=s_wm[:, 5:6], op0=AluOpType.mult, op1=AluOpType.mult), inc="V")
                    for g0 in range(0, 32, 4):
                        pr.wait("tensor", "V")
                        for j in range(4):
                            pr.op("tensor", lambda e, g0=g0, j=j: e.transpose(
                                out=ps_tb[:, j * 128:(j + 1) * 128],
                                in_=guq[:, (g0 + j) * 128:(g0 + j + 1) * 128],
                                identity=s_idb[:]), inc="P")
                        pm = pr.val("P")
                        pr.wait("vector", "P", pm)
                        pr.op("vector", lambda e, g0=g0, i=i: e.tensor_copy(
                            out=guT[:, g0:g0 + 4, i * 128:(i + 1) * 128],
                            in_=ps_tb.ap().rearrange("p (a b) -> p a b", b=128)), inc="V")

            with contextlib.ExitStack() as c6:
                e6 = c6.enter_context
                wds = e6(nc.sbuf_tensor(f"wds{ck}", [128, 32, 512], BF16))
                ps_d = e6(nc.psum_tensor(f"ps_d{ck}", [128, 512], F32))

                for o2 in range(2):
                    pr.wait("sync", "P")
                    lw = dma(wds[:], io["wdq_sc"][:, o2 * 512:(o2 + 1) * 512]
                             .rearrange("(c p) n -> p c n", p=128))
                    for i in range(4):
                        pr.wait("tensor", "L", lw)
                        pr.wait("tensor", "V")
                        for c in range(32):
                            pr.op("tensor", lambda e, c=c, i=i: e.matmul(
                                out=ps_d[:], lhsT=guT[:, c, i * 128:(i + 1) * 128],
                                rhs=wds[:, c, :], start=(c == 0), stop=(c == 31)))
                        pm = pr.mark("tensor", "P")
                        pr.wait("vector", "P", pm)
                        pr.op("vector", lambda e, i=i, o2=o2: e.scalar_tensor_tensor(
                            out=h2[:, i, o2 * 512:(o2 + 1) * 512], in0=ps_d[:],
                            scalar=fd[:, i:i + 1], in1=h1[:, i, o2 * 512:(o2 + 1) * 512],
                            op0=AluOpType.mult, op1=AluOpType.add), inc="V")

            for i in range(4):
                pr.op("vector", lambda e, i=i: e.bn_stats(out=stats[:, 0:6], in_=h2[:, i, 0:512]))
                pr.op("vector", lambda e, i=i: e.bn_stats(out=stats[:, 6:12], in_=h2[:, i, 512:1024]), inc="V")
                vbounce()
                vm = pr.op("vector", lambda e: e.bn_aggr(out=mv[:], in_=stats[:]), inc="V")
                pr.wait("scalar", "V", vm)
                am = pr.op("scalar", lambda e: e.activation(
                    out=lnt[:, 0:1], in_=mv[:, 1:2], func=AF.Ln, bias=1e-5), inc="A")
                pr.wait("vector", "A", am)
                vm = pr.op("vector", lambda e: e.tensor_copy(out=lnt[:, 1:2], in_=lnt[:, 0:1]), inc="V")
                pr.wait("scalar", "V", vm)
                am = pr.op("scalar", lambda e: e.activation(
                    out=lnt[:, 2:3], in_=lnt[:, 1:2], func=AF.Exp, scale=-0.5), inc="A")
                pr.wait("vector", "A", am)
                if prev_store:
                    pr.wait("vector", "S", prev_store)
                vm = pr.op("vector", lambda e, i=i: e.tensor_scalar(
                    out=outs[:], in0=h2[:, i, :], scalar1=mv[:, 0:1], scalar2=lnt[:, 2:3],
                    op0=AluOpType.subtract, op1=AluOpType.mult), inc="V")
                pr.wait("sync", "V", vm)
                prev_store = store(io["y"][(4 * ck + i) * 128:(4 * ck + i + 1) * 128, :], outs[:])

    pr.wait("sync", "S")


def _rope_tables(pos):
    j = np.arange(32, dtype=np.float64)
    theta = 10000.0 ** (-j / 32.0)
    ang = pos.astype(np.float64)[:, None] * theta[None, :]
    c = np.cos(ang)
    s = np.sin(ang)
    C64 = np.empty((pos.shape[0], 64), np.float64)
    S64 = np.empty((pos.shape[0], 64), np.float64)
    C64[:, 0::2] = c
    C64[:, 1::2] = c
    S64[:, 0::2] = -s
    S64[:, 1::2] = s
    C = np.tile(C64, (1, 8)).astype(np.float32)
    S = np.tile(S64, (1, 8)).astype(np.float32)
    return np.ascontiguousarray(C), np.ascontiguousarray(S)


def _host_inputs(inputs, core):
    b, r = core // 2, core % 2
    x = np.asarray(inputs["L0_attn_x"], np.float32)[b]
    kvmap = kv_slot_map(r)
    x_kv = np.zeros((NKV * 128, D), np.float32)
    pos = np.zeros(NKV * 128, np.float32)
    for s, m in enumerate(kvmap):
        if m >= 0:
            x_kv[s * 128:(s + 1) * 128] = x[m * 128:(m + 1) * 128]
            pos[s * 128:(s + 1) * 128] = np.arange(m * 128, (m + 1) * 128)
    xT = np.ascontiguousarray(x_kv.T)
    ropeC, ropeS = _rope_tables(pos)
    qt = q_rows_tiles(r)
    xh = np.concatenate([x[m * 128:(m + 1) * 128] for m in qt], 0)
    f32 = lambda a: np.ascontiguousarray(np.asarray(a, np.float32))
    d = {
        "xT": xT, "ropeC": ropeC, "ropeS": ropeS,
        "wqT": f32(np.asarray(inputs["L0_attn_W_q"]).T),
        "wkT": f32(np.asarray(inputs["L0_attn_W_k"]).T),
        "wvT": f32(np.asarray(inputs["L0_attn_W_v"]).T),
        "woT": f32(np.asarray(inputs["L0_attn_W_o"]).T),
        "wgTf": f32(np.asarray(inputs["L0_ffn_w_gate"]).T),
        "wuTf": f32(np.asarray(inputs["L0_ffn_w_up"]).T),
        "wdTf": f32(np.asarray(inputs["L0_ffn_w_down"]).T),
        "bqr": f32(np.asarray(inputs["L0_attn_b_q"]).reshape(1, D)),
        "bkr": f32(np.asarray(inputs["L0_attn_b_k"]).reshape(1, D)),
        "bvr": f32(np.asarray(inputs["L0_attn_b_v"]).reshape(1, D)),
        "bor": f32(np.asarray(inputs["L0_attn_b_o"]).reshape(1, D)),
        "xh": np.ascontiguousarray(xh),
        "identf": np.eye(128, dtype=np.float32),
        "identb": np.eye(128, dtype=np.float32).astype(bf16),
        "onesrow": np.ones((1, 128), np.float32),
        "onescol": np.ones((128, 1), np.float32),
        "scph": np.full((128, 1), 0.125 if r == 0 else 0.0, np.float32),
        "phA": np.full((128, 1), 1.0 if r == 0 else 0.0, np.float32),
        "phB": np.full((128, 1), 0.0 if r == 0 else -40.0, np.float32),
        "cst": np.tile(np.asarray(CONSTS, np.float32), (128, 1)),
    }
    return d


def kernel(**inputs):
    nc = build_program()
    in_maps = [_host_inputs(inputs, c) for c in range(8)]
    res = run_bass_kernel_spmd(nc, in_maps, core_ids=list(range(8)))
    out = np.zeros((B, T, D), np.float32)
    for c in range(8):
        b, r = c // 2, c % 2
        yc = np.asarray(res.results[c]["y"], np.float32)
        for idx, m in enumerate(q_rows_tiles(r)):
            out[b, m * 128:(m + 1) * 128] = yc[idx * 128:(idx + 1) * 128]
    return out


# version 02
# speedup vs baseline: 1.0433x; 1.0433x over previous
import contextlib

import numpy as np
import ml_dtypes

import concourse.bass as bass
import concourse.mybir as mybir
from concourse.alu_op_type import AluOpType
from concourse.bass_utils import run_bass_kernel_spmd

F32 = mybir.dt.float32
BF16 = mybir.dt.bfloat16
I32 = mybir.dt.int32
AF = mybir.ActivationFunctionType
bf16 = ml_dtypes.bfloat16

B, T, D = 4, 4096, 1024
NKV = 32
NG = 4
INNER = 4096
LN127 = float(np.log(127.0))
LN8 = float(np.log(8.0))
CONSTS = (0.0, 1e-5, -LN8, LN127)

DEV = {"taps": False, "do_p0": True, "do_p1": True, "do_p2": True, "do_p3": True, "ng": NG, "nkv": NKV, "halves": 2, "do_bounds": True, "do_norms": True, "do_rope": True, "do_tr": True, "p2_half": -1, "h1_kv": 32, "h1_wload": 1, "h1_bounds": 1}


class Prog:
    ENGINES = ("sync", "tensor", "scalar", "vector", "gpsimd")

    def __init__(self, nc, ctx):
        self.nc = nc
        self.ops = {e: [] for e in self.ENGINES}
        self.sems = {}
        self.last_wait = {e: {} for e in self.ENGINES}
        for name in ("L", "S", "P", "A", "V", "G"):
            h = ctx.enter_context(nc.semaphore(name))
            self.sems[name] = [h, 0]

    def val(self, s):
        return self.sems[s][1]

    def wait(self, eng, s, v=None):
        if v is None:
            v = self.sems[s][1]
        if v is None or v <= 0:
            return
        if self.last_wait[eng].get(s, -1) >= v:
            return
        self.last_wait[eng][s] = v
        h = self.sems[s][0]
        self.ops[eng].append(lambda e, h=h, v=v: e.wait_ge(h, v))

    def op(self, eng, fn, inc=None, k=1):
        if inc is None:
            self.ops[eng].append(fn)
            return None
        h = self.sems[inc][0]
        self.sems[inc][1] += k
        self.ops[eng].append(lambda e, fn=fn, h=h, k=k: fn(e).then_inc(h, k))
        return self.sems[inc][1]

    def mark(self, eng, sem, k=1):
        h = self.sems[sem][0]
        self.sems[sem][1] += k
        prev = self.ops[eng].pop()
        self.ops[eng].append(lambda e, p=prev, h=h, k=k: p(e).then_inc(h, k))
        return self.sems[sem][1]


def kv_slot_map(role):
    if role == 0:
        return list(range(32))
    m = [-1] * 32
    for k in range(4):
        for j in range(4):
            m[8 * k + 4 + j] = 8 * k + j
    for k in range(1, 4):
        for j in range(4):
            m[8 * k + j] = 8 * k - 4 + j
    return m


def q_rows_tiles(role):
    m = kv_slot_map(role)
    return [m[8 * G + 4 + i] for G in range(4) for i in range(4)]


def build_program(dev=None):
    dv = dict(DEV)
    if dev:
        dv.update(dev)
    ng = dv["ng"]
    nkv = dv["nkv"]

    nc = bass.Bass("TRN2", target_bir_lowering=False)
    dt = nc.dram_tensor
    io = {}
    io["xT"] = dt("xT", [D, NKV * 128], F32, kind="ExternalInput").ap()
    io["ropeC"] = dt("ropeC", [NKV * 128, 512], F32, kind="ExternalInput").ap()
    io["ropeS"] = dt("ropeS", [NKV * 128, 512], F32, kind="ExternalInput").ap()
    for n, shp in (("wqT", [D, D]), ("wkT", [D, D]), ("wvT", [D, D]),
                   ("woT", [D, D]), ("wgTf", [D, INNER]), ("wuTf", [D, INNER]),
                   ("wdTf", [INNER, D]), ("xh", [2048, D]),
                   ("bqr", [1, D]), ("bkr", [1, D]), ("bvr", [1, D]), ("bor", [1, D]),
                   ("identf", [128, 128]), ("onesrow", [1, 128]), ("onescol", [128, 1]),
                   ("scph", [128, 1]), ("phA", [128, 1]), ("phB", [128, 1]),
                   ("cst", [128, 4])):
        io[n] = dt(n, shp, F32, kind="ExternalInput").ap()
    io["identb"] = dt("identb", [128, 128], BF16, kind="ExternalInput").ap()

    io["y"] = dt("y", [2048, D], F32, kind="ExternalOutput").ap()
    kind_sc = "ExternalOutput" if dv["taps"] else "Internal"
    io["attn_sc"] = dt("attn_sc", [2048, D], BF16, kind=kind_sc).ap()
    if dv["taps"]:
        io["tap_qt"] = dt("tap_qt", [128, 4, 2048], BF16, kind="ExternalOutput").ap()
        io["tap_kt"] = dt("tap_kt", [128, 4, NKV * 128], BF16, kind="ExternalOutput").ap()
        io["tap_vs"] = dt("tap_vs", [128, NKV, 8, 65], BF16, kind="ExternalOutput").ap()
        io["tap_nm"] = dt("tap_nm", [128, 8], F32, kind="ExternalOutput").ap()
        io["tap_sc2"] = dt("tap_sc2", [2048, D], BF16, kind="ExternalOutput").ap()
    kind_w = "ExternalOutput" if dv["taps"] else "Internal"
    io["wgq_sc"] = dt("wgq_sc", [D, INNER], BF16, kind=kind_w).ap()
    io["wuq_sc"] = dt("wuq_sc", [D, INNER], BF16, kind=kind_w).ap()
    io["wdq_sc"] = dt("wdq_sc", [INNER, D], BF16, kind=kind_w).ap()
    if dv["taps"]:
        io["tap_h1"] = dt("tap_h1", [2048, D], F32, kind="ExternalOutput").ap()
        io["tap_gu"] = dt("tap_gu", [2048, INNER], F32, kind="ExternalOutput").ap()
        io["tap_p0"] = dt("tap_p0", [128, 40], F32, kind="ExternalOutput").ap()

    with contextlib.ExitStack() as ctx:
        pr = Prog(nc, ctx)
        _body(nc, pr, ctx, io, dv)
        block = ctx.enter_context(nc.Block())

        @block.sync
        def _(e):
            for fn in pr.ops["sync"]:
                fn(e)

        @block.tensor
        def _(e):
            for fn in pr.ops["tensor"]:
                fn(e)

        @block.scalar
        def _(e):
            for fn in pr.ops["scalar"]:
                fn(e)

        @block.vector
        def _(e):
            for fn in pr.ops["vector"]:
                fn(e)

        @block.gpsimd
        def _(e):
            for fn in pr.ops["gpsimd"]:
                fn(e)

    return nc


def _body(nc, pr, ctx, io, dv):
    ng, nkv = dv["ng"], dv["nkv"]
    sb = lambda name, shape, dtype: ctx.enter_context(nc.sbuf_tensor(name, shape, dtype))

    def dma(dst, src):
        return pr.op("sync", lambda e, d=dst, s=src: e.dma_start(out=d, in_=s), inc="L", k=16)

    def store(dst, src):
        return pr.op("sync", lambda e, d=dst, s=src: e.dma_start(
            out=d, in_=s, allow_slow_non_contiguous=True), inc="S", k=16)

    def vbounce():
        pr.wait("gpsimd", "V")
        gb = pr.op("gpsimd", lambda e: e.engine_nop(), inc="G")
        pr.wait("vector", "G", gb)

    s_idf = sb("s_idf", [128, 128], F32)
    s_idb = sb("s_idb", [128, 128], BF16)
    s_ones = sb("s_ones", [1, 128], F32)
    s_onesc = sb("s_onesc", [128, 1], F32)
    s_bq = sb("s_bq", [1, D], F32)
    s_bk = sb("s_bk", [1, D], F32)
    s_bv = sb("s_bv", [1, D], F32)
    s_bo = sb("s_bo", [1, D], F32)
    s_scph = sb("s_scph", [128, 1], F32)
    s_phA = sb("s_phA", [128, 1], F32)
    s_phB = sb("s_phB", [128, 1], F32)
    s_wm = sb("s_wm", [128, 6], F32)
    s_negM = sb("s_negM", [128, 4], F32)
    s_negM2 = sb("s_negM2", [128, 4], F32)
    s_cst = sb("s_cst", [128, 4], F32)

    for n, d_ in (("identf", s_idf), ("identb", s_idb), ("onesrow", s_ones),
                  ("onescol", s_onesc), ("bqr", s_bq), ("bkr", s_bk),
                  ("bvr", s_bv), ("bor", s_bo), ("scph", s_scph),
                  ("phA", s_phA), ("phB", s_phB), ("cst", s_cst)):
        lcst = dma(d_[:], io[n][:])
    for ci, cv in enumerate(CONSTS):
        nc.const_aps.register(F32, cv, s_cst[:, ci:ci + 1])
    for eng in ("tensor", "scalar", "vector", "gpsimd"):
        pr.wait(eng, "L", lcst)

    if not dv["do_p0"]:
        pr.wait("sync", "S")
        if not dv["do_p1"]:
            return
    if dv["do_p0"]:
     with contextlib.ExitStack() as c0:
        e0 = c0.enter_context
        w_ld = [e0(nc.sbuf_tensor(f"w_ld{i}", [128, 4096], F32)) for i in range(2)]
        w_i32 = e0(nc.sbuf_tensor("w_i32", [128, 4096], I32))
        w_q = [e0(nc.sbuf_tensor(f"w_q{i}", [128, 4096], BF16)) for i in range(2)]
        sums = e0(nc.sbuf_tensor("sums", [128, 32], F32))
        rsum = e0(nc.sbuf_tensor("rsum", [128, 1], F32))
        msc = e0(nc.sbuf_tensor("msc", [1, 4], F32))
        ps_m = e0(nc.psum_tensor("ps_m", [128, 2], F32))

        for wi, (wf, wsc, cols, nchunk) in enumerate((
                (io["wgTf"], io["wgq_sc"], 4096, 8),
                (io["wuTf"], io["wuq_sc"], 4096, 8),
                (io["wdTf"], io["wdq_sc"], 1024, 32))):
            nelem = float(D * INNER)
            red_marks = {}
            for ci in range(nchunk):
                buf = w_ld[ci % 2]
                pr.wait("sync", "V", red_marks.get(ci - 2))
                lm = dma(buf[:, 0:cols], wf[ci * 128:(ci + 1) * 128, :])
                pr.wait("vector", "L", lm)
                red_marks[ci] = pr.op("vector", lambda e, b=buf, c=cols, ci=ci: e.reduce_sum(
                    out=sums[:, ci:ci + 1], in_=b[:, 0:c],
                    axis=mybir.AxisListType.X, apply_absolute_value=True), inc="V")
            bounce = pr.val("V")
            pr.wait("gpsimd", "V", bounce)
            gb = pr.op("gpsimd", lambda e: e.engine_nop(), inc="G")
            pr.wait("vector", "G", gb)
            vm = pr.op("vector", lambda e, n=nchunk: e.reduce_sum(
                out=rsum[:], in_=sums[:, 0:n], axis=mybir.AxisListType.X), inc="V")
            pr.wait("tensor", "V", vm)
            pm = pr.op("tensor", lambda e: e.matmul(
                out=ps_m[0:1, 0:1], lhsT=s_onesc[:], rhs=rsum[:],
                start=True, stop=True), inc="P")
            pr.wait("vector", "P", pm)
            vm = pr.op("vector", lambda e, n=nelem: e.tensor_scalar(
                out=msc[0:1, 0:1], in0=ps_m[0:1, 0:1], scalar1=1.0 / n,
                scalar2=1e-5, op0=AluOpType.mult, op1=AluOpType.max), inc="V")
            pr.wait("scalar", "V", vm)
            am = pr.op("scalar", lambda e: e.activation(
                out=msc[0:1, 1:2], in_=msc[0:1, 0:1], func=AF.Ln), inc="A")
            pr.wait("vector", "A", am)
            vm = pr.op("vector", lambda e: e.tensor_copy(
                out=msc[0:1, 2:3], in_=msc[0:1, 1:2]), inc="V")
            pr.wait("scalar", "V", vm)
            am = pr.op("scalar", lambda e: e.activation(
                out=msc[0:1, 3:4], in_=msc[0:1, 2:3], func=AF.Exp, scale=-1.0), inc="A")
            pr.wait("tensor", "A", am)
            pm = pr.op("tensor", lambda e: e.matmul(
                out=ps_m[:, 0:1], lhsT=s_ones[:], rhs=msc[0:1, 3:4],
                start=True, stop=True), inc="P")
            pr.wait("vector", "P", pm)
            pr.op("vector", lambda e, w=wi: e.tensor_copy(
                out=s_wm[:, 2 * w:2 * w + 1], in_=ps_m[:, 0:1]), inc="V")
            vm = pr.val("V")
            pr.wait("tensor", "V", vm)
            pm = pr.op("tensor", lambda e: e.matmul(
                out=ps_m[:, 1:2], lhsT=s_ones[:], rhs=msc[0:1, 0:1],
                start=True, stop=True), inc="P")
            pr.wait("vector", "P", pm)
            vm = pr.op("vector", lambda e, w=wi: e.tensor_copy(
                out=s_wm[:, 2 * w + 1:2 * w + 2], in_=ps_m[:, 1:2]), inc="V")

            if dv["taps"] and wi == 0:
                pr.wait("sync", "V")
                pr.wait("sync", "A")
                store(io["tap_p0"][:, 0:32], sums[:])
                store(io["tap_p0"][:, 32:33], rsum[:])
                store(io["tap_p0"][0:1, 33:37], msc[:])
                store(io["tap_p0"][:, 37:38], s_wm[:, 0:1])
            st_marks = {}
            q_marks = {}
            for ci in range(nchunk):
                buf = w_ld[ci % 2]
                qb = w_q[ci % 2]
                pr.wait("sync", "V", q_marks.get(ci - 2))
                lm = dma(buf[:, 0:cols], wf[ci * 128:(ci + 1) * 128, :])
                pr.wait("vector", "L", lm)
                pr.wait("vector", "S", st_marks.get(ci - 2))
                pr.op("vector", lambda e, b=buf, c=cols, w=wi: e.tensor_scalar(
                    out=b[:, 0:c], in0=b[:, 0:c], scalar1=s_wm[:, 2 * w:2 * w + 1],
                    scalar2=None, op0=AluOpType.mult), inc="V")
                pr.op("vector", lambda e, b=buf, c=cols: e.tensor_copy(
                    out=w_i32[:, 0:c], in_=b[:, 0:c]), inc="V")
                vm = pr.op("vector", lambda e, q=qb, c=cols: e.tensor_scalar(
                    out=q[:, 0:c], in0=w_i32[:, 0:c], scalar1=1.0, scalar2=-1.0,
                    op0=AluOpType.min, op1=AluOpType.max), inc="V")
                q_marks[ci] = vm
                pr.wait("sync", "V", vm)
                st_marks[ci] = store(wsc[ci * 128:(ci + 1) * 128, :], qb[:, 0:cols])

    for eng in ("sync", "tensor", "scalar", "vector", "gpsimd"):
        pr.wait(eng, "S")

    if not dv["do_p1"]:
        pr.wait("sync", "S")
        return

    for half in range(dv["halves"]):
        hoff = half * 512
        with contextlib.ExitStack() as c1:
            e1 = c1.enter_context
            QT = e1(nc.sbuf_tensor(f"QT{half}", [128, 4, ng * 512], BF16))
            KT = e1(nc.sbuf_tensor(f"KT{half}", [128, 4, nkv * 128], BF16))
            Vsb = e1(nc.sbuf_tensor(f"Vsb{half}", [128, nkv, 8, 65], BF16))
            outn = e1(nc.sbuf_tensor(f"outn{half}", [128, 4 * ng, 512], BF16))
            racc = e1(nc.sbuf_tensor(f"racc{half}", [128, 8], F32))
            nbuf = e1(nc.sbuf_tensor(f"nbuf{half}", [128, 4, 48], F32))
            bscr = e1(nc.sbuf_tensor(f"bscr{half}", [4, 8], F32))
            brow = e1(nc.sbuf_tensor(f"brow{half}", [1, 4], F32))

            with contextlib.ExitStack() as c2:
                e2 = c2.enter_context
                wkv = e2(nc.sbuf_tensor(f"wkv{half}", [128, 8, 1024], BF16))
                wqb = e2(nc.sbuf_tensor(f"wqb{half}", [128, 8, 512], BF16))
                wldf = e2(nc.sbuf_tensor(f"wldf{half}", [128, 512], F32))
                xtl = [e2(nc.sbuf_tensor(f"xtl{half}_{i}", [128, 8, 128], F32)) for i in range(2)]
                xtb = [e2(nc.sbuf_tensor(f"xtb{half}_{i}", [128, 8, 128], BF16)) for i in range(2)]
                rC = [e2(nc.sbuf_tensor(f"rC{half}_{i}", [128, 512], F32)) for i in range(2)]
                rS = [e2(nc.sbuf_tensor(f"rS{half}_{i}", [128, 512], F32)) for i in range(2)]
                ktok = e2(nc.sbuf_tensor(f"ktok{half}", [128, 512], F32))
                qtok = e2(nc.sbuf_tensor(f"qtok{half}", [128, 512], F32))
                m1 = e2(nc.sbuf_tensor(f"m1{half}", [128, 512], F32))
                m2 = e2(nc.sbuf_tensor(f"m2{half}", [128, 512], F32))
                krope = e2(nc.sbuf_tensor(f"krope{half}", [128, 512], BF16))
                qrope = e2(nc.sbuf_tensor(f"qrope{half}", [128, 512], BF16))
                sqs = e2(nc.sbuf_tensor(f"sqs{half}", [128, 512], F32))
                red4 = e2(nc.sbuf_tensor(f"red4{half}", [128, 4], F32))
                ps_k = [e2(nc.psum_tensor(f"ps_k{half}_{i}", [128, 512], F32)) for i in range(2)]
                ps_v = [e2(nc.psum_tensor(f"ps_v{half}_{i}", [128, 512], F32)) for i in range(2)]
                ps_q = [e2(nc.psum_tensor(f"ps_q{half}_{i}", [128, 512], F32)) for i in range(2)]
                ps_trb = e2(nc.psum_tensor(f"ps_trb{half}", [128, 512], BF16))
                ps_tr4 = e2(nc.psum_tensor(f"ps_tr4{half}", [128, 256], F32))

                wload_list = (((io["wkT"], 0), (io["wvT"], 1), (io["wqT"], 2))
                              if (half == 0 or dv["h1_wload"]) else ())
                for wi, (wsrc, wdst) in enumerate(wload_list):
                    for i in range(8):
                        pr.wait("sync", "V")
                        lm = dma(wldf[:], wsrc[i * 128:(i + 1) * 128, hoff:hoff + 512])
                        pr.wait("vector", "L", lm)
                        if wdst == 0:
                            pr.op("vector", lambda e, i=i: e.tensor_copy(
                                out=wkv[:, i, 0:512], in_=wldf[:]), inc="V")
                        elif wdst == 1:
                            pr.op("vector", lambda e, i=i: e.tensor_copy(
                                out=wkv[:, i, 512:1024], in_=wldf[:]), inc="V")
                        else:
                            pr.op("vector", lambda e, i=i: e.tensor_copy(
                                out=wqb[:, i, :], in_=wldf[:]), inc="V")

                gm_ones = pr.op("gpsimd", lambda e: e.memset(Vsb[:], 1.0), inc="G")
                pr.op("vector", lambda e: e.memset(racc[:], 0.0))
                pr.op("vector", lambda e: e.memset(nbuf[:], 0.0), inc="V")

                mk = {}
                nkv_h = nkv if half == 0 else min(nkv, dv["h1_kv"])
                for s in range(nkv_h):
                    bi = s % 2
                    is_q = (s % 8) >= 4
                    G = s // 8
                    qcol = G * 512 + ((s % 8) - 4) * 128

                    pr.wait("sync", "V", mk.get(("cast", s - 2)))
                    lx = dma(xtl[bi][:], io["xT"][:, s * 128:(s + 1) * 128]
                             .rearrange("(c p) t -> p c t", p=128))
                    pr.wait("sync", "G", mk.get(("rope", s - 2)))
                    dma(rC[bi][:], io["ropeC"][s * 128:(s + 1) * 128, :])
                    ls = dma(rS[bi][:], io["ropeS"][s * 128:(s + 1) * 128, :])
                    pr.wait("vector", "L", lx)
                    pr.wait("vector", "P", mk.get(("pm", s - 2)))
                    mk[("cast", s)] = pr.op("vector", lambda e, b=bi: e.tensor_copy(
                        out=xtb[b][:], in_=xtl[b][:]), inc="V")

                    pr.wait("tensor", "V", mk[("cast", s)])
                    pr.wait("tensor", "V", mk.get(("pevac", s - 2)))
                    pr.op("tensor", lambda e, ho=hoff, pa=bi: e.matmul(
                        out=ps_k[pa][:], lhsT=s_ones[:], rhs=s_bk[0:1, ho:ho + 512],
                        start=True, stop=False))
                    pr.op("tensor", lambda e, ho=hoff, pa=bi: e.matmul(
                        out=ps_v[pa][:], lhsT=s_ones[:], rhs=s_bv[0:1, ho:ho + 512],
                        start=True, stop=False))
                    for i in range(8):
                        pr.op("tensor", lambda e, b=bi, i=i, pa=bi: e.matmul(
                            out=ps_k[pa][:], lhsT=xtb[b][:, i, :], rhs=wkv[:, i, 0:512],
                            start=False, stop=(i == 7)))
                        pr.op("tensor", lambda e, b=bi, i=i, pa=bi: e.matmul(
                            out=ps_v[pa][:], lhsT=xtb[b][:, i, :], rhs=wkv[:, i, 512:1024],
                            start=False, stop=(i == 7)))
                    if is_q:
                        pr.op("tensor", lambda e, ho=hoff, pa=bi: e.matmul(
                            out=ps_q[pa][:], lhsT=s_ones[:], rhs=s_bq[0:1, ho:ho + 512],
                            start=True, stop=False))
                        for i in range(8):
                            pr.op("tensor", lambda e, b=bi, i=i, pa=bi: e.matmul(
                                out=ps_q[pa][:], lhsT=xtb[b][:, i, :], rhs=wqb[:, i, :],
                                start=False, stop=(i == 7)))
                    mk[("pm", s)] = pr.mark("tensor", "P")

                    pr.wait("vector", "P", mk[("pm", s)])
                    if s == 0:
                        pr.wait("vector", "G", gm_ones)
                    pr.wait("vector", "G", mk.get(("rope", s - 1)))
                    pr.op("vector", lambda e, pa=bi: e.tensor_copy(out=ktok[:], in_=ps_k[pa][:]), inc="V")
                    pr.op("vector", lambda e, s=s, pa=bi: e.tensor_copy(
                        out=Vsb[:, s, :, 0:64],
                        in_=ps_v[pa].ap().rearrange("p (h d) -> p h d", d=64)), inc="V")
                    if is_q:
                        pr.op("vector", lambda e, pa=bi: e.tensor_copy(out=qtok[:], in_=ps_q[pa][:]), inc="V")
                    mk[("pevac", s)] = pr.val("V")

                    pr.wait("gpsimd", "V", mk[("pevac", s)])
                    pr.wait("gpsimd", "L", ls)
                    pr.wait("gpsimd", "P", mk.get(("tpm", s - 1)))
                    pr.wait("gpsimd", "A", mk.get(("sq", s - 1)))

                    def rope(tin, tout, b=bi):
                        pr.op("gpsimd", lambda e, t=tin, b=b: e.tensor_tensor(
                            out=m1[:], in0=t[:], in1=rC[b][:], op=AluOpType.mult))
                        pr.op("gpsimd", lambda e, t=tin, b=b: e.tensor_tensor(
                            out=m2[:, 0::2], in0=t[:, 1::2], in1=rS[b][:, 0::2], op=AluOpType.mult))
                        pr.op("gpsimd", lambda e, t=tin, b=b: e.tensor_tensor(
                            out=m2[:, 1::2], in0=t[:, 0::2], in1=rS[b][:, 1::2], op=AluOpType.mult))
                        return pr.op("gpsimd", lambda e, o=tout: e.tensor_tensor(
                            out=o[:], in0=m1[:], in1=m2[:], op=AluOpType.add), inc="G")

                    gk = rope(ktok, krope)
                    gq = rope(qtok, qrope) if is_q else gk
                    mk[("rope", s)] = gq

                    if not dv["do_norms"]:
                        mk[("sq", s)] = 0
                        mk[("red", s)] = 0
                    else:
                        pr.wait("scalar", "G", gk)
                        pr.wait("scalar", "V", mk.get(("red", s - 1)))
                        amk = pr.op("scalar", lambda e: e.activation(
                            out=sqs[:], in_=krope[:], func=AF.Square), inc="A")
                        pr.wait("vector", "A", amk)
                        vr = pr.op("vector", lambda e, s=s: e.tensor_reduce(
                            out=nbuf[:, :, s:s + 1],
                            in_=sqs.ap().rearrange("p (h d) -> p h d", d=128),
                            axis=mybir.AxisListType.X, op=AluOpType.add), inc="V")
                        if is_q:
                            qi = 32 + (s // 8) * 4 + (s % 8 - 4)
                            pr.wait("scalar", "V", vr)
                            pr.wait("scalar", "G", mk[("rope", s)])
                            amq = pr.op("scalar", lambda e: e.activation(
                                out=sqs[:], in_=qrope[:], func=AF.Square), inc="A")
                            pr.wait("vector", "A", amq)
                            vr = pr.op("vector", lambda e, qi=qi: e.tensor_reduce(
                                out=nbuf[:, :, qi:qi + 1],
                                in_=sqs.ap().rearrange("p (h d) -> p h d", d=128),
                                axis=mybir.AxisListType.X, op=AluOpType.add), inc="V")
                            mk[("sq", s)] = amq
                        else:
                            mk[("sq", s)] = amk
                        mk[("red", s)] = vr

                    pr.wait("tensor", "G", mk[("rope", s)])
                    pr.wait("tensor", "V", mk.get(("tevac", s - 1)))
                    for j in range(4):
                        pr.op("tensor", lambda e, j=j: e.transpose(
                            out=ps_trb[:, j * 128:(j + 1) * 128],
                            in_=krope[:, j * 128:(j + 1) * 128], identity=s_idb[:]), inc="P")
                    pm = pr.val("P")
                    pr.wait("vector", "P", pm)
                    pr.op("vector", lambda e, s=s: e.tensor_copy(
                        out=KT[:, :, s * 128:(s + 1) * 128],
                        in_=ps_trb.ap().rearrange("p (h t) -> p h t", t=128)), inc="V")
                    tm = pr.val("V")
                    if is_q:
                        pr.wait("tensor", "V", tm)
                        for j in range(4):
                            pr.op("tensor", lambda e, j=j: e.transpose(
                                out=ps_trb[:, j * 128:(j + 1) * 128],
                                in_=qrope[:, j * 128:(j + 1) * 128], identity=s_idb[:]), inc="P")
                        pm = pr.val("P")
                        pr.wait("vector", "P", pm)
                        pr.op("vector", lambda e, q=qcol: e.tensor_copy(
                            out=QT[:, :, q:q + 128],
                            in_=ps_trb.ap().rearrange("p (h t) -> p h t", t=128)), inc="V")
                        tm = pr.val("V")
                    mk[("tevac", s)] = tm
                    mk[("tpm", s)] = pm

                if not dv["do_bounds"] or (half == 1 and not dv["h1_bounds"]):
                    continue
                vbounce()
                pr.op("vector", lambda e: e.tensor_reduce(
                    out=racc.ap().rearrange("p (a b) -> p a b", b=1)[:, 0:4, :],
                    in_=nbuf[:, :, 32:48], axis=mybir.AxisListType.X,
                    op=AluOpType.max))
                pr.op("vector", lambda e: e.tensor_reduce(
                    out=racc.ap().rearrange("p (a b) -> p a b", b=1)[:, 4:8, :],
                    in_=nbuf[:, :, 0:32], axis=mybir.AxisListType.X,
                    op=AluOpType.max), inc="V")
                vbounce()
                pr.op("vector", lambda e: e.tensor_scalar(
                    out=racc[:], in0=racc[:], scalar1=1e-30, scalar2=None,
                    op0=AluOpType.max), inc="V")
                pr.wait("tensor", "V")
                pr.op("tensor", lambda e: e.transpose(
                    out=ps_tr4[0:4, 0:128], in_=racc[:, 0:4], identity=s_idf[:]), inc="P")
                pm = pr.op("tensor", lambda e: e.transpose(
                    out=ps_tr4[0:4, 128:256], in_=racc[:, 4:8], identity=s_idf[:]), inc="P")
                pr.wait("vector", "P", pm)
                pr.op("vector", lambda e: e.tensor_reduce(
                    out=bscr[0:4, 0:1], in_=ps_tr4[0:4, 0:128],
                    axis=mybir.AxisListType.X, op=AluOpType.max), inc="V")
                vm = pr.op("vector", lambda e: e.tensor_reduce(
                    out=bscr[0:4, 1:2], in_=ps_tr4[0:4, 128:256],
                    axis=mybir.AxisListType.X, op=AluOpType.max), inc="V")
                pr.wait("scalar", "V", vm)
                am = pr.op("scalar", lambda e: e.activation(
                    out=bscr[0:4, 2:3], in_=bscr[0:4, 0:1], func=AF.Ln), inc="A")
                pr.wait("vector", "A", am)
                vm = pr.op("vector", lambda e: e.tensor_copy(
                    out=bscr[0:4, 4:5], in_=bscr[0:4, 2:3]), inc="V")
                pr.wait("scalar", "V", vm)
                am = pr.op("scalar", lambda e: e.activation(
                    out=bscr[0:4, 3:4], in_=bscr[0:4, 1:2], func=AF.Ln), inc="A")
                pr.wait("vector", "A", am)
                vm = pr.op("vector", lambda e: e.tensor_tensor(
                    out=bscr[0:4, 5:6], in0=bscr[0:4, 4:5], in1=bscr[0:4, 3:4],
                    op=AluOpType.add), inc="V")
                pr.wait("scalar", "V", vm)
                am = pr.op("scalar", lambda e: e.activation(
                    out=bscr[0:4, 6:7], in_=bscr[0:4, 5:6], func=AF.Exp,
                    scale=0.5, bias=-LN8), inc="A")
                pr.wait("vector", "A", am)
                vm = pr.op("vector", lambda e: e.tensor_scalar(
                    out=bscr[0:4, 7:8], in0=bscr[0:4, 6:7], scalar1=-1.0,
                    scalar2=None, op0=AluOpType.mult), inc="V")
                pr.wait("tensor", "V", vm)
                pm = pr.op("tensor", lambda e: e.transpose(
                    out=ps_tr4[0:1, 0:4], in_=bscr[0:4, 7:8],
                    identity=s_idf[0:4, 0:4]), inc="P")
                pr.wait("vector", "P", pm)
                vm = pr.op("vector", lambda e: e.tensor_copy(
                    out=brow[0:1, 0:4], in_=ps_tr4[0:1, 0:4]), inc="V")
                pr.wait("tensor", "V", vm)
                pm = pr.op("tensor", lambda e: e.matmul(
                    out=ps_tr4[:, 0:4], lhsT=s_ones[:], rhs=brow[0:1, 0:4],
                    start=True, stop=True), inc="P")
                pr.wait("vector", "P", pm)
                pr.op("vector", lambda e: e.tensor_copy(out=s_negM[:], in_=ps_tr4[:, 0:4]), inc="V")
                vm = pr.op("vector", lambda e: e.tensor_scalar(
                    out=s_negM2[:], in0=s_negM[:], scalar1=s_phA[:, 0:1],
                    scalar2=s_phB[:, 0:1], op0=AluOpType.mult, op1=AluOpType.add), inc="V")
                pr.wait("scalar", "V", vm)
                pr.wait("tensor", "V", vm)

                if dv["taps"] and half == 0:
                    pr.wait("sync", "V", vm)
                    store(io["tap_qt"][:], QT[:])
                    store(io["tap_kt"][:], KT[:])
                    store(io["tap_vs"][:], Vsb[:])
                    store(io["tap_nm"][:, 0:4], s_negM[:])
                    store(io["tap_nm"][:, 4:8], s_negM2[:])

            if not dv["do_p2"]:
                continue
            if dv["p2_half"] >= 0 and half != dv["p2_half"]:
                continue

            with contextlib.ExitStack() as c3:
                e3 = c3.enter_context
                ps_S = [e3(nc.psum_tensor(f"ps_S{half}_{i}", [128, 1024], F32)) for i in range(2)]
                ps_oA = e3(nc.psum_tensor(f"ps_oA{half}", [128, 512], F32))
                ps_oB = e3(nc.psum_tensor(f"ps_oB{half}", [128, 512], F32))
                ps_to = e3(nc.psum_tensor(f"ps_to{half}", [128, 260], F32))
                Pt = [e3(nc.sbuf_tensor(f"Pt{half}_{i}", [128, 1024], BF16)) for i in range(2)]
                o_raw = e3(nc.sbuf_tensor(f"o_raw{half}", [65, 2, 512], F32))
                tok = e3(nc.sbuf_tensor(f"tok{half}", [128, 2, 4, 65], F32))
                rcl = e3(nc.sbuf_tensor(f"rcl{half}", [128, 8], F32))
                rc = e3(nc.sbuf_tensor(f"rc{half}", [128, 8], F32))

                exp_mark = [0, 0]
                pv_mark = [0, 0]
                score_mark = [0, 0]
                oraw_trans_mark = 0
                oraw_copy_prev = 0
                last_norm = 0
                for hp in range(4):
                    for G in range(ng):
                        E = 8 * (G + 1)
                        for j in range(E):
                            pa = j % 2
                            slj = (8 * G + 7 - j) if j < 4 else (j - 4)
                            pr.wait("tensor", "A", exp_mark[pa])
                            for h in range(2):
                                pr.op("tensor", lambda e, h=h, s=slj, pa=pa, hp=hp, G=G: e.matmul(
                                    out=ps_S[pa][:, h * 512:(h + 1) * 512],
                                    lhsT=KT[64 * h:64 * h + 64, hp, s * 128:(s + 1) * 128],
                                    rhs=QT[64 * h:64 * h + 64, hp, G * 512:(G + 1) * 512],
                                    start=True, stop=True,
                                    tile_position=(64 * h, 0)))
                            score_mark[pa] = pr.mark("tensor", "P")
                            pr.wait("scalar", "P", score_mark[pa])
                            pr.wait("scalar", "P", pv_mark[pa])
                            if 4 <= j < 8:
                                am = pr.op("scalar", lambda e, hp=hp, pa=pa: e.activation(
                                    out=Pt[pa][:], in_=ps_S[pa][:], func=AF.Exp,
                                    bias=s_negM2[:, hp:hp + 1], scale=s_scph[:, 0:1]), inc="A")
                            else:
                                am = pr.op("scalar", lambda e, hp=hp, pa=pa: e.activation(
                                    out=Pt[pa][:], in_=ps_S[pa][:], func=AF.Exp,
                                    bias=s_negM[:, hp:hp + 1], scale=0.125), inc="A")
                            exp_mark[pa] = am
                            dep_s, dep_v = "A", am
                            if j < 4:
                                pr.wait("gpsimd", "A", am)
                                gm = pr.op("gpsimd", lambda e, pa=pa, base=-384 + 128 * j: e.affine_select(
                                    out=Pt[pa].ap().rearrange("p (x c) -> p x c", c=512),
                                    in_=Pt[pa].ap().rearrange("p (x c) -> p x c", c=512),
                                    pattern=[[0, 2], [1, 512]],
                                    compare_op=AluOpType.is_ge, fill=0.0,
                                    base=base, channel_multiplier=-1), inc="G")
                                dep_s, dep_v = "G", gm
                            pr.wait("tensor", dep_s, dep_v)
                            if j == 0:
                                pr.wait("tensor", "A", oraw_copy_prev)
                            first = (j == 0)
                            last = (j == E - 1)
                            for h in range(2):
                                bank = ps_oA if h == 0 else ps_oB
                                hh = 2 * hp + h
                                pr.op("tensor", lambda e, b=bank, s=slj, hh=hh, pa=pa, h=h,
                                      st=first, sp=last: e.matmul(
                                    out=b[0:65, :], lhsT=Vsb[:, s, hh, 0:65],
                                    rhs=Pt[pa][:, h * 512:(h + 1) * 512], start=st, stop=sp))
                            pv_mark[pa] = pr.mark("tensor", "P")

                        pr.wait("scalar", "P", max(pv_mark))
                        pr.wait("scalar", "P", oraw_trans_mark)
                        pr.wait("scalar", "V", last_norm)
                        for h in range(2):
                            bank = ps_oA if h == 0 else ps_oB
                            pr.op("scalar", lambda e, b=bank, h=h: e.copy(
                                out=o_raw[0:65, h, :], in_=b[0:65, :]))
                        am = pr.mark("scalar", "A")
                        oraw_copy = am
                        oraw_copy_prev = am
                        ev_prev = 0
                        for h in range(2):
                            pr.wait("tensor", "A", oraw_copy)
                            pr.wait("tensor", "V", ev_prev)
                            for i in range(4):
                                pr.op("tensor", lambda e, h=h, i=i: e.transpose(
                                    out=ps_to[:, i * 65:(i + 1) * 65],
                                    in_=o_raw[0:65, h, i * 128:(i + 1) * 128],
                                    identity=s_idf[0:65, 0:65]), inc="P")
                            pm = pr.val("P")
                            if h == 1:
                                oraw_trans_mark = pm
                            pr.wait("vector", "P", pm)
                            ev_prev = pr.op("vector", lambda e, h=h: e.tensor_copy(
                                out=tok[:, h, :, :],
                                in_=ps_to.ap().rearrange("p (i c) -> p i c", c=65)), inc="V")
                        pr.wait("scalar", "V", ev_prev)
                        am = pr.op("scalar", lambda e: e.activation(
                            out=rcl[:], in_=tok[:, :, :, 64:65].rearrange("p h i c -> p (h i c)"),
                            func=AF.Ln), inc="A")
                        pr.wait("vector", "A", am)
                        vm = pr.op("vector", lambda e: e.tensor_copy(out=rc[:], in_=rcl[:]), inc="V")
                        pr.wait("scalar", "V", vm)
                        am = pr.op("scalar", lambda e: e.activation(
                            out=rc[:], in_=rc[:], func=AF.Exp, scale=-1.0), inc="A")
                        pr.wait("vector", "A", am)
                        for h in range(2):
                            for i in range(4):
                                hh = 2 * hp + h
                                pr.op("vector", lambda e, h=h, i=i, hh=hh, G=G: e.tensor_scalar(
                                    out=outn[:, 4 * G + i, hh * 64:(hh + 1) * 64],
                                    in0=tok[:, h, i, 0:64],
                                    scalar1=rc[:, 4 * h + i:4 * h + i + 1], scalar2=None,
                                    op0=AluOpType.mult), inc="V")
                        last_norm = pr.val("V")
                pr.wait("sync", "V", last_norm)
                store(io["attn_sc"].rearrange("(a p) c -> p a c", p=128)[:, :, hoff:hoff + 512],
                      outn[:])
                if dv["taps"]:
                    store(io["tap_sc2"].rearrange("(a p) c -> p a c", p=128)[:, :, hoff:hoff + 512],
                          outn[:])
                for eng in ("sync", "tensor", "scalar", "vector", "gpsimd"):
                    pr.wait(eng, "S")

    if not dv["do_p3"]:
        pr.wait("sync", "S")
        return

    pr.wait("sync", "S")
    with contextlib.ExitStack() as c4:
        e4 = c4.enter_context
        h1 = e4(nc.sbuf_tensor("h1", [128, 4, 1024], F32))
        h2 = h1
        xq = e4(nc.sbuf_tensor("xq", [128, 1024], BF16))
        xqT = e4(nc.sbuf_tensor("xqT", [128, 8, 512], BF16))
        guT = e4(nc.sbuf_tensor("guT", [128, 32, 512], BF16))
        fg = e4(nc.sbuf_tensor("fg", [128, 4], F32))
        fu = e4(nc.sbuf_tensor("fu", [128, 4], F32))
        fd = e4(nc.sbuf_tensor("fd", [128, 4], F32))
        lnt = e4(nc.sbuf_tensor("lnt", [128, 16], F32))
        stats = e4(nc.sbuf_tensor("stats", [128, 12], F32))
        mv = e4(nc.sbuf_tensor("mv", [128, 2], F32))
        h1n = e4(nc.sbuf_tensor("h1n", [128, 1024], F32))
        hi32 = e4(nc.sbuf_tensor("hi32", [128, 1024], I32))
        xht = e4(nc.sbuf_tensor("xht", [128, 1024], F32))
        outs = e4(nc.sbuf_tensor("outs", [128, 1024], F32))
        ps_y = e4(nc.psum_tensor("ps_y", [128, 1024], F32))
        ps_tb = e4(nc.psum_tensor("ps_tb", [128, 512], BF16))

        def transpose_1024(src_ap_fn, dst, dst_col, n=8):
            for g0 in range(0, n, 4):
                pr.wait("tensor", "V")
                for j in range(4):
                    pr.op("tensor", lambda e, g0=g0, j=j: e.transpose(
                        out=ps_tb[:, j * 128:(j + 1) * 128],
                        in_=src_ap_fn(g0 + j), identity=s_idb[:]), inc="P")
                pm = pr.val("P")
                pr.wait("vector", "P", pm)
                pr.op("vector", lambda e, g0=g0: e.tensor_copy(
                    out=dst[:, g0:g0 + 4, dst_col:dst_col + 128],
                    in_=ps_tb.ap().rearrange("p (a b) -> p a b", b=128)), inc="V")

        nstore = 0
        prev_store = 0
        for ck in range(4):
            cA = contextlib.ExitStack()
            aol = cA.enter_context(nc.sbuf_tensor(f"aol{ck}", [128, 1024], BF16))
            aoT = cA.enter_context(nc.sbuf_tensor(f"aoT{ck}", [128, 8, 512], BF16))
            woTb = cA.enter_context(nc.sbuf_tensor(f"woTb{ck}", [128, 8, 1024], BF16))
            with nc.sbuf_tensor(f"wldo{ck}", [128, 1024], F32) as wldo:
                for i in range(8):
                    pr.wait("sync", "V")
                    lm = dma(wldo[:], io["woT"][i * 128:(i + 1) * 128, :])
                    pr.wait("vector", "L", lm)
                    pr.op("vector", lambda e, i=i: e.tensor_copy(out=woTb[:, i, :], in_=wldo[:]), inc="V")
            for i in range(4):
                pr.wait("sync", "V")
                lm = dma(aol[:], io["attn_sc"][(4 * ck + i) * 128:(4 * ck + i + 1) * 128, :])
                pr.wait("tensor", "L", lm)
                transpose_1024(lambda j: aol[:, j * 128:(j + 1) * 128], aoT, i * 128)

            for i in range(4):
                pr.wait("sync", "V")
                lxh = dma(xht[:], io["xh"][(4 * ck + i) * 128:(4 * ck + i + 1) * 128, :])
                pr.wait("tensor", "V")
                pr.op("tensor", lambda e: e.matmul(
                    out=ps_y[:, 0:512], lhsT=s_ones[:], rhs=s_bo[0:1, 0:512],
                    start=True, stop=False))
                pr.op("tensor", lambda e: e.matmul(
                    out=ps_y[:, 512:1024], lhsT=s_ones[:], rhs=s_bo[0:1, 512:1024],
                    start=True, stop=False))
                for c in range(8):
                    pr.op("tensor", lambda e, c=c, i=i: e.matmul(
                        out=ps_y[:, 0:512], lhsT=aoT[:, c, i * 128:(i + 1) * 128],
                        rhs=woTb[:, c, 0:512], start=False, stop=(c == 7)))
                    pr.op("tensor", lambda e, c=c, i=i: e.matmul(
                        out=ps_y[:, 512:1024], lhsT=aoT[:, c, i * 128:(i + 1) * 128],
                        rhs=woTb[:, c, 512:1024], start=False, stop=(c == 7)))
                pm = pr.mark("tensor", "P")
                pr.wait("vector", "P", pm)
                pr.wait("vector", "L", lxh)
                vm_h1 = pr.op("vector", lambda e, i=i: e.scalar_tensor_tensor(
                    out=h1[:, i, :], in0=ps_y[:], scalar=0.0, in1=xht[:],
                    op0=AluOpType.add, op1=AluOpType.add), inc="V")
                if dv["taps"]:
                    pr.wait("sync", "V", vm_h1)
                    store(io["tap_h1"][(4 * ck + i) * 128:(4 * ck + i + 1) * 128, :],
                          h1[:, i, :])

                pr.op("vector", lambda e, i=i: e.bn_stats(out=stats[:, 0:6], in_=h1[:, i, 0:512]))
                pr.op("vector", lambda e, i=i: e.bn_stats(out=stats[:, 6:12], in_=h1[:, i, 512:1024]), inc="V")
                vbounce()
                vm = pr.op("vector", lambda e: e.bn_aggr(out=mv[:], in_=stats[:]), inc="V")
                pr.wait("scalar", "V", vm)
                am = pr.op("scalar", lambda e: e.activation(
                    out=lnt[:, 0:1], in_=mv[:, 1:2], func=AF.Ln, bias=1e-5), inc="A")
                pr.wait("vector", "A", am)
                vm = pr.op("vector", lambda e: e.tensor_copy(out=lnt[:, 1:2], in_=lnt[:, 0:1]), inc="V")
                pr.wait("scalar", "V", vm)
                am = pr.op("scalar", lambda e: e.activation(
                    out=lnt[:, 2:3], in_=lnt[:, 1:2], func=AF.Exp, scale=-0.5), inc="A")
                pr.wait("vector", "A", am)
                pr.op("vector", lambda e, i=i: e.tensor_scalar(
                    out=h1n[:], in0=h1[:, i, :], scalar1=mv[:, 0:1], scalar2=lnt[:, 2:3],
                    op0=AluOpType.subtract, op1=AluOpType.mult), inc="V")
                vq = pr.op("vector", lambda e: e.tensor_reduce(
                    out=lnt[:, 3:4], in_=h1n[:], axis=mybir.AxisListType.X,
                    op=AluOpType.max, apply_absolute_value=True), inc="V")
                pr.wait("gpsimd", "V", vq)
                gq_ = pr.op("gpsimd", lambda e: e.tensor_scalar(
                    out=lnt[:, 4:5], in0=lnt[:, 3:4], scalar1=1e-5, scalar2=None,
                    op0=AluOpType.max), inc="G")
                pr.wait("scalar", "G", gq_)
                pr.wait("vector", "G", gq_)
                am = pr.op("scalar", lambda e: e.activation(
                    out=lnt[:, 5:6], in_=lnt[:, 4:5], func=AF.Ln), inc="A")
                pr.wait("vector", "A", am)
                vm = pr.op("vector", lambda e: e.tensor_copy(out=lnt[:, 6:7], in_=lnt[:, 5:6]), inc="V")
                pr.wait("scalar", "V", vm)
                am = pr.op("scalar", lambda e: e.activation(
                    out=lnt[:, 7:8], in_=lnt[:, 6:7], func=AF.Exp, scale=-1.0,
                    bias=LN127), inc="A")
                pr.wait("vector", "A", am)
                pr.op("vector", lambda e: e.tensor_scalar(
                    out=h1n[:], in0=h1n[:], scalar1=lnt[:, 7:8], scalar2=None,
                    op0=AluOpType.mult), inc="V")
                pr.op("vector", lambda e: e.tensor_copy(out=hi32[:], in_=h1n[:]), inc="V")
                vm = pr.op("vector", lambda e: e.tensor_copy(
                    out=xq[:], in_=hi32[:, 0:1024]), inc="V")
                pr.op("vector", lambda e, i=i: e.tensor_scalar(
                    out=fg[:, i:i + 1], in0=lnt[:, 4:5], scalar1=1.0 / 127.0,
                    scalar2=s_wm[:, 1:2], op0=AluOpType.mult, op1=AluOpType.mult))
                pr.op("vector", lambda e, i=i: e.tensor_scalar(
                    out=fu[:, i:i + 1], in0=lnt[:, 4:5], scalar1=1.0 / 127.0,
                    scalar2=s_wm[:, 3:4], op0=AluOpType.mult, op1=AluOpType.mult), inc="V")
                transpose_1024(lambda j: xq[:, j * 128:(j + 1) * 128], xqT, i * 128)

            cA.close()
            with contextlib.ExitStack() as c5:
                e5 = c5.enter_context
                gu = e5(nc.sbuf_tensor(f"gu{ck}", [128, 4, 4096], F32))
                guq = e5(nc.sbuf_tensor(f"guq{ck}", [128, 4096], BF16))
                wgs = [e5(nc.sbuf_tensor(f"wgs{ck}_{i}", [128, 8, 512], BF16)) for i in range(2)]
                wus = [e5(nc.sbuf_tensor(f"wus{ck}_{i}", [128, 8, 512], BF16)) for i in range(2)]
                gsb = e5(nc.sbuf_tensor(f"gsb{ck}", [128, 512], F32))
                usb = e5(nc.sbuf_tensor(f"usb{ck}", [128, 512], F32))
                ps_g = [e5(nc.psum_tensor(f"ps_g{ck}_{i}", [128, 512], F32)) for i in range(2)]
                ps_u = [e5(nc.psum_tensor(f"ps_u{ck}_{i}", [128, 512], F32)) for i in range(2)]

                wl = {}
                evac_m = {}
                gu_m = {}
                pm_m = {}
                for o in range(8):
                    if o == 0:
                        wl[0] = (dma(wgs[0][:], io["wgq_sc"][:, 0:512]
                                     .rearrange("(c p) n -> p c n", p=128)),
                                 dma(wus[0][:], io["wuq_sc"][:, 0:512]
                                     .rearrange("(c p) n -> p c n", p=128)))
                    if o + 1 < 8:
                        pr.wait("sync", "P", pm_m.get((o - 1, 3), 0))
                        wl[o + 1] = (dma(wgs[(o + 1) % 2][:],
                                         io["wgq_sc"][:, (o + 1) * 512:(o + 2) * 512]
                                         .rearrange("(c p) n -> p c n", p=128)),
                                     dma(wus[(o + 1) % 2][:],
                                         io["wuq_sc"][:, (o + 1) * 512:(o + 2) * 512]
                                         .rearrange("(c p) n -> p c n", p=128)))
                    ob = o % 2
                    for i in range(4):
                        pa = i % 2
                        prev = (o, i - 2) if i >= 2 else (o - 1, i + 2)
                        pr.wait("tensor", "L", wl[o][1])
                        pr.wait("tensor", "A", evac_m.get(("g",) + prev, 0))
                        pr.wait("tensor", "V", evac_m.get(("u",) + prev, 0))
                        for c in range(8):
                            pr.op("tensor", lambda e, c=c, i=i, ob=ob, pa=pa: e.matmul(
                                out=ps_g[pa][:], lhsT=xqT[:, c, i * 128:(i + 1) * 128],
                                rhs=wgs[ob][:, c, :], start=(c == 0), stop=(c == 7)))
                            pr.op("tensor", lambda e, c=c, i=i, ob=ob, pa=pa: e.matmul(
                                out=ps_u[pa][:], lhsT=xqT[:, c, i * 128:(i + 1) * 128],
                                rhs=wus[ob][:, c, :], start=(c == 0), stop=(c == 7)))
                        pm = pr.mark("tensor", "P")
                        pm_m[(o, i)] = pm
                        pr.wait("scalar", "P", pm)
                        lastidx = (o, i - 1) if i >= 1 else (o - 1, 3)
                        pr.wait("scalar", "V", gu_m.get(lastidx, 0))
                        am = pr.op("scalar", lambda e, i=i, pa=pa: e.activation(
                            out=gsb[:], in_=ps_g[pa][:], func=AF.Silu,
                            scale=fg[:, i:i + 1]), inc="A")
                        evac_m[("g", o, i)] = am
                        pr.wait("vector", "P", pm)
                        vm_u = pr.op("vector", lambda e, i=i, pa=pa: e.tensor_scalar(
                            out=usb[:], in0=ps_u[pa][:], scalar1=fu[:, i:i + 1],
                            scalar2=None, op0=AluOpType.mult), inc="V")
                        evac_m[("u", o, i)] = vm_u
                        pr.wait("vector", "A", am)
                        gu_m[(o, i)] = pr.op("vector", lambda e, i=i, o=o: e.tensor_tensor(
                            out=gu[:, i, o * 512:(o + 1) * 512], in0=gsb[:], in1=usb[:],
                            op=AluOpType.mult), inc="V")

                if dv["taps"]:
                    pr.wait("sync", "V")
                    for i4 in range(4):
                        store(io["tap_gu"][(4 * ck + i4) * 128:(4 * ck + i4 + 1) * 128, :],
                              gu[:, i4, :])
                for i in range(4):
                    vq = pr.op("vector", lambda e, i=i: e.tensor_reduce(
                        out=lnt[:, 8:9], in_=gu[:, i, :], axis=mybir.AxisListType.X,
                        op=AluOpType.max, apply_absolute_value=True), inc="V")
                    pr.wait("gpsimd", "V", vq)
                    gq_ = pr.op("gpsimd", lambda e: e.tensor_scalar(
                        out=lnt[:, 9:10], in0=lnt[:, 8:9], scalar1=1e-5, scalar2=None,
                        op0=AluOpType.max), inc="G")
                    pr.wait("scalar", "G", gq_)
                    pr.wait("vector", "G", gq_)
                    am = pr.op("scalar", lambda e: e.activation(
                        out=lnt[:, 10:11], in_=lnt[:, 9:10], func=AF.Ln), inc="A")
                    pr.wait("vector", "A", am)
                    vm = pr.op("vector", lambda e: e.tensor_copy(
                        out=lnt[:, 11:12], in_=lnt[:, 10:11]), inc="V")
                    pr.wait("scalar", "V", vm)
                    am = pr.op("scalar", lambda e: e.activation(
                        out=lnt[:, 12:13], in_=lnt[:, 11:12], func=AF.Exp, scale=-1.0,
                        bias=LN127), inc="A")
                    pr.wait("vector", "A", am)
                    pr.op("vector", lambda e, i=i: e.tensor_scalar(
                        out=gu[:, i, :], in0=gu[:, i, :], scalar1=lnt[:, 12:13],
                        scalar2=None, op0=AluOpType.mult), inc="V")
                    for q4 in range(4):
                        pr.op("vector", lambda e, i=i, q4=q4: e.tensor_copy(
                            out=hi32[:], in_=gu[:, i, q4 * 1024:(q4 + 1) * 1024]), inc="V")
                        vm = pr.op("vector", lambda e, q4=q4: e.tensor_copy(
                            out=guq[:, q4 * 1024:(q4 + 1) * 1024], in_=hi32[:]), inc="V")
                    pr.op("vector", lambda e, i=i: e.tensor_scalar(
                        out=fd[:, i:i + 1], in0=lnt[:, 9:10], scalar1=1.0 / 127.0,
                        scalar2=s_wm[:, 5:6], op0=AluOpType.mult, op1=AluOpType.mult), inc="V")
                    for g0 in range(0, 32, 4):
                        pr.wait("tensor", "V")
                        for j in range(4):
                            pr.op("tensor", lambda e, g0=g0, j=j: e.transpose(
                                out=ps_tb[:, j * 128:(j + 1) * 128],
                                in_=guq[:, (g0 + j) * 128:(g0 + j + 1) * 128],
                                identity=s_idb[:]), inc="P")
                        pm = pr.val("P")
                        pr.wait("vector", "P", pm)
                        pr.op("vector", lambda e, g0=g0, i=i: e.tensor_copy(
                            out=guT[:, g0:g0 + 4, i * 128:(i + 1) * 128],
                            in_=ps_tb.ap().rearrange("p (a b) -> p a b", b=128)), inc="V")

            with contextlib.ExitStack() as c6:
                e6 = c6.enter_context
                wds = e6(nc.sbuf_tensor(f"wds{ck}", [128, 32, 512], BF16))
                ps_d = e6(nc.psum_tensor(f"ps_d{ck}", [128, 512], F32))

                for o2 in range(2):
                    pr.wait("sync", "P")
                    lw = dma(wds[:], io["wdq_sc"][:, o2 * 512:(o2 + 1) * 512]
                             .rearrange("(c p) n -> p c n", p=128))
                    for i in range(4):
                        pr.wait("tensor", "L", lw)
                        pr.wait("tensor", "V")
                        for c in range(32):
                            pr.op("tensor", lambda e, c=c, i=i: e.matmul(
                                out=ps_d[:], lhsT=guT[:, c, i * 128:(i + 1) * 128],
                                rhs=wds[:, c, :], start=(c == 0), stop=(c == 31)))
                        pm = pr.mark("tensor", "P")
                        pr.wait("vector", "P", pm)
                        pr.op("vector", lambda e, i=i, o2=o2: e.scalar_tensor_tensor(
                            out=h2[:, i, o2 * 512:(o2 + 1) * 512], in0=ps_d[:],
                            scalar=fd[:, i:i + 1], in1=h1[:, i, o2 * 512:(o2 + 1) * 512],
                            op0=AluOpType.mult, op1=AluOpType.add), inc="V")

            for i in range(4):
                pr.op("vector", lambda e, i=i: e.bn_stats(out=stats[:, 0:6], in_=h2[:, i, 0:512]))
                pr.op("vector", lambda e, i=i: e.bn_stats(out=stats[:, 6:12], in_=h2[:, i, 512:1024]), inc="V")
                vbounce()
                vm = pr.op("vector", lambda e: e.bn_aggr(out=mv[:], in_=stats[:]), inc="V")
                pr.wait("scalar", "V", vm)
                am = pr.op("scalar", lambda e: e.activation(
                    out=lnt[:, 0:1], in_=mv[:, 1:2], func=AF.Ln, bias=1e-5), inc="A")
                pr.wait("vector", "A", am)
                vm = pr.op("vector", lambda e: e.tensor_copy(out=lnt[:, 1:2], in_=lnt[:, 0:1]), inc="V")
                pr.wait("scalar", "V", vm)
                am = pr.op("scalar", lambda e: e.activation(
                    out=lnt[:, 2:3], in_=lnt[:, 1:2], func=AF.Exp, scale=-0.5), inc="A")
                pr.wait("vector", "A", am)
                if prev_store:
                    pr.wait("vector", "S", prev_store)
                vm = pr.op("vector", lambda e, i=i: e.tensor_scalar(
                    out=outs[:], in0=h2[:, i, :], scalar1=mv[:, 0:1], scalar2=lnt[:, 2:3],
                    op0=AluOpType.subtract, op1=AluOpType.mult), inc="V")
                pr.wait("sync", "V", vm)
                prev_store = store(io["y"][(4 * ck + i) * 128:(4 * ck + i + 1) * 128, :], outs[:])

    pr.wait("sync", "S")


def _rope_tables(pos):
    j = np.arange(32, dtype=np.float64)
    theta = 10000.0 ** (-j / 32.0)
    ang = pos.astype(np.float64)[:, None] * theta[None, :]
    c = np.cos(ang)
    s = np.sin(ang)
    C64 = np.empty((pos.shape[0], 64), np.float64)
    S64 = np.empty((pos.shape[0], 64), np.float64)
    C64[:, 0::2] = c
    C64[:, 1::2] = c
    S64[:, 0::2] = -s
    S64[:, 1::2] = s
    C = np.tile(C64, (1, 8)).astype(np.float32)
    S = np.tile(S64, (1, 8)).astype(np.float32)
    return np.ascontiguousarray(C), np.ascontiguousarray(S)


def _host_inputs(inputs, core):
    b, r = core // 2, core % 2
    x = np.asarray(inputs["L0_attn_x"], np.float32)[b]
    kvmap = kv_slot_map(r)
    x_kv = np.zeros((NKV * 128, D), np.float32)
    pos = np.zeros(NKV * 128, np.float32)
    for s, m in enumerate(kvmap):
        if m >= 0:
            x_kv[s * 128:(s + 1) * 128] = x[m * 128:(m + 1) * 128]
            pos[s * 128:(s + 1) * 128] = np.arange(m * 128, (m + 1) * 128)
    xT = np.ascontiguousarray(x_kv.T)
    ropeC, ropeS = _rope_tables(pos)
    qt = q_rows_tiles(r)
    xh = np.concatenate([x[m * 128:(m + 1) * 128] for m in qt], 0)
    f32 = lambda a: np.ascontiguousarray(np.asarray(a, np.float32))
    d = {
        "xT": xT, "ropeC": ropeC, "ropeS": ropeS,
        "wqT": f32(np.asarray(inputs["L0_attn_W_q"]).T),
        "wkT": f32(np.asarray(inputs["L0_attn_W_k"]).T),
        "wvT": f32(np.asarray(inputs["L0_attn_W_v"]).T),
        "woT": f32(np.asarray(inputs["L0_attn_W_o"]).T),
        "wgTf": f32(np.asarray(inputs["L0_ffn_w_gate"]).T),
        "wuTf": f32(np.asarray(inputs["L0_ffn_w_up"]).T),
        "wdTf": f32(np.asarray(inputs["L0_ffn_w_down"]).T),
        "bqr": f32(np.asarray(inputs["L0_attn_b_q"]).reshape(1, D)),
        "bkr": f32(np.asarray(inputs["L0_attn_b_k"]).reshape(1, D)),
        "bvr": f32(np.asarray(inputs["L0_attn_b_v"]).reshape(1, D)),
        "bor": f32(np.asarray(inputs["L0_attn_b_o"]).reshape(1, D)),
        "xh": np.ascontiguousarray(xh),
        "identf": np.eye(128, dtype=np.float32),
        "identb": np.eye(128, dtype=np.float32).astype(bf16),
        "onesrow": np.ones((1, 128), np.float32),
        "onescol": np.ones((128, 1), np.float32),
        "scph": np.full((128, 1), 0.125 if r == 0 else 0.0, np.float32),
        "phA": np.full((128, 1), 1.0 if r == 0 else 0.0, np.float32),
        "phB": np.full((128, 1), 0.0 if r == 0 else -40.0, np.float32),
        "cst": np.tile(np.asarray(CONSTS, np.float32), (128, 1)),
    }
    return d


def kernel(**inputs):
    nc = build_program()
    in_maps = [_host_inputs(inputs, c) for c in range(8)]
    res = run_bass_kernel_spmd(nc, in_maps, core_ids=list(range(8)))
    out = np.zeros((B, T, D), np.float32)
    for c in range(8):
        b, r = c // 2, c % 2
        yc = np.asarray(res.results[c]["y"], np.float32)
        for idx, m in enumerate(q_rows_tiles(r)):
            out[b, m * 128:(m + 1) * 128] = yc[idx * 128:(idx + 1) * 128]
    return out


# version 04
# speedup vs baseline: 1.3143x; 1.2598x over previous
import contextlib

import numpy as np
import ml_dtypes

import concourse.bass as bass
import concourse.mybir as mybir
from concourse.alu_op_type import AluOpType
from concourse.bass_utils import run_bass_kernel_spmd

F32 = mybir.dt.float32
BF16 = mybir.dt.bfloat16
I32 = mybir.dt.int32
AF = mybir.ActivationFunctionType
bf16 = ml_dtypes.bfloat16

B, T, D = 4, 4096, 1024
NKV = 32
NG = 4
INNER = 4096
LN127 = float(np.log(127.0))
LN8 = float(np.log(8.0))
CONSTS = (0.0, 1e-5, -LN8, LN127)

DEV = {"taps": False, "do_p0": True, "do_p1": True, "do_p2": True, "do_p3": True, "ng": NG, "nkv": NKV, "halves": 2, "do_bounds": True, "do_norms": True, "do_rope": True, "do_tr": True, "p2_half": -1, "h1_kv": 32, "h1_wload": 1, "h1_bounds": 1}


class Prog:
    ENGINES = ("sync", "tensor", "scalar", "vector", "gpsimd")

    def __init__(self, nc, ctx):
        self.nc = nc
        self.ops = {e: [] for e in self.ENGINES}
        self.sems = {}
        self.last_wait = {e: {} for e in self.ENGINES}
        for name in ("L", "S", "P", "A", "V", "G"):
            h = ctx.enter_context(nc.semaphore(name))
            self.sems[name] = [h, 0]

    def val(self, s):
        return self.sems[s][1]

    def wait(self, eng, s, v=None):
        if v is None:
            v = self.sems[s][1]
        if v is None or v <= 0:
            return
        if self.last_wait[eng].get(s, -1) >= v:
            return
        self.last_wait[eng][s] = v
        h = self.sems[s][0]
        self.ops[eng].append(lambda e, h=h, v=v: e.wait_ge(h, v))

    def op(self, eng, fn, inc=None, k=1):
        if inc is None:
            self.ops[eng].append(fn)
            return None
        h = self.sems[inc][0]
        self.sems[inc][1] += k
        self.ops[eng].append(lambda e, fn=fn, h=h, k=k: fn(e).then_inc(h, k))
        return self.sems[inc][1]

    def mark(self, eng, sem, k=1):
        h = self.sems[sem][0]
        self.sems[sem][1] += k
        prev = self.ops[eng].pop()
        self.ops[eng].append(lambda e, p=prev, h=h, k=k: p(e).then_inc(h, k))
        return self.sems[sem][1]


def kv_slot_map(role):
    if role == 0:
        return list(range(32))
    m = [-1] * 32
    for k in range(4):
        for j in range(4):
            m[8 * k + 4 + j] = 8 * k + j
    for k in range(1, 4):
        for j in range(4):
            m[8 * k + j] = 8 * k - 4 + j
    return m


def q_rows_tiles(role):
    m = kv_slot_map(role)
    return [m[8 * G + 4 + i] for G in range(4) for i in range(4)]


def build_program(dev=None):
    dv = dict(DEV)
    if dev:
        dv.update(dev)
    ng = dv["ng"]
    nkv = dv["nkv"]

    nc = bass.Bass("TRN2", target_bir_lowering=False)
    dt = nc.dram_tensor
    io = {}
    io["xT"] = dt("xT", [D, NKV * 128], F32, kind="ExternalInput").ap()
    io["ropeC"] = dt("ropeC", [NKV * 128, 512], F32, kind="ExternalInput").ap()
    io["ropeS"] = dt("ropeS", [NKV * 128, 512], F32, kind="ExternalInput").ap()
    for n, shp in (("wqT", [D, D]), ("wkT", [D, D]), ("wvT", [D, D]),
                   ("woT", [D, D]), ("wgTf", [D, INNER]), ("wuTf", [D, INNER]),
                   ("wdTf", [INNER, D]), ("xh", [2048, D]),
                   ("bqr", [1, D]), ("bkr", [1, D]), ("bvr", [1, D]), ("bor", [1, D]),
                   ("identf", [128, 128]), ("onesrow", [1, 128]), ("onescol", [128, 1]),
                   ("scph", [128, 1]), ("phA", [128, 1]), ("phB", [128, 1]),
                   ("cst", [128, 4])):
        io[n] = dt(n, shp, F32, kind="ExternalInput").ap()
    io["identb"] = dt("identb", [128, 128], BF16, kind="ExternalInput").ap()

    io["y"] = dt("y", [2048, D], F32, kind="ExternalOutput").ap()
    kind_sc = "ExternalOutput" if dv["taps"] else "Internal"
    io["attn_sc"] = dt("attn_sc", [2048, D], BF16, kind=kind_sc).ap()
    if dv["taps"]:
        io["tap_qt"] = dt("tap_qt", [128, 4, 2048], BF16, kind="ExternalOutput").ap()
        io["tap_kt"] = dt("tap_kt", [128, 4, NKV * 128], BF16, kind="ExternalOutput").ap()
        io["tap_vs"] = dt("tap_vs", [128, NKV, 8, 65], BF16, kind="ExternalOutput").ap()
        io["tap_nm"] = dt("tap_nm", [128, 8], F32, kind="ExternalOutput").ap()
        io["tap_sc2"] = dt("tap_sc2", [2048, D], BF16, kind="ExternalOutput").ap()
    kind_w = "ExternalOutput" if dv["taps"] else "Internal"
    io["wgq_sc"] = dt("wgq_sc", [D, INNER], BF16, kind=kind_w).ap()
    io["wuq_sc"] = dt("wuq_sc", [D, INNER], BF16, kind=kind_w).ap()
    io["wdq_sc"] = dt("wdq_sc", [INNER, D], BF16, kind=kind_w).ap()
    if dv["taps"]:
        io["tap_h1"] = dt("tap_h1", [2048, D], F32, kind="ExternalOutput").ap()
        io["tap_gu"] = dt("tap_gu", [2048, INNER], F32, kind="ExternalOutput").ap()
        io["tap_p0"] = dt("tap_p0", [128, 40], F32, kind="ExternalOutput").ap()

    with contextlib.ExitStack() as ctx:
        pr = Prog(nc, ctx)
        _body(nc, pr, ctx, io, dv)
        block = ctx.enter_context(nc.Block())

        @block.sync
        def _(e):
            for fn in pr.ops["sync"]:
                fn(e)

        @block.tensor
        def _(e):
            for fn in pr.ops["tensor"]:
                fn(e)

        @block.scalar
        def _(e):
            for fn in pr.ops["scalar"]:
                fn(e)

        @block.vector
        def _(e):
            for fn in pr.ops["vector"]:
                fn(e)

        @block.gpsimd
        def _(e):
            for fn in pr.ops["gpsimd"]:
                fn(e)

    return nc


def _body(nc, pr, ctx, io, dv):
    ng, nkv = dv["ng"], dv["nkv"]
    sb = lambda name, shape, dtype: ctx.enter_context(nc.sbuf_tensor(name, shape, dtype))

    def dma(dst, src):
        return pr.op("sync", lambda e, d=dst, s=src: e.dma_start(out=d, in_=s), inc="L", k=16)

    def store(dst, src):
        return pr.op("sync", lambda e, d=dst, s=src: e.dma_start(
            out=d, in_=s, allow_slow_non_contiguous=True), inc="S", k=16)

    def vbounce():
        pr.wait("gpsimd", "V")
        gb = pr.op("gpsimd", lambda e: e.engine_nop(), inc="G")
        pr.wait("vector", "G", gb)

    s_idf = sb("s_idf", [128, 128], F32)
    s_idb = sb("s_idb", [128, 128], BF16)
    s_ones = sb("s_ones", [1, 128], F32)
    s_onesc = sb("s_onesc", [128, 1], F32)
    s_bq = sb("s_bq", [1, D], F32)
    s_bk = sb("s_bk", [1, D], F32)
    s_bv = sb("s_bv", [1, D], F32)
    s_bo = sb("s_bo", [1, D], F32)
    s_scph = sb("s_scph", [128, 1], F32)
    s_phA = sb("s_phA", [128, 1], F32)
    s_phB = sb("s_phB", [128, 1], F32)
    s_wm = sb("s_wm", [128, 6], F32)
    s_negM = sb("s_negM", [128, 4], F32)
    s_negM2 = sb("s_negM2", [128, 4], F32)
    s_cst = sb("s_cst", [128, 4], F32)

    for n, d_ in (("identf", s_idf), ("identb", s_idb), ("onesrow", s_ones),
                  ("onescol", s_onesc), ("bqr", s_bq), ("bkr", s_bk),
                  ("bvr", s_bv), ("bor", s_bo), ("scph", s_scph),
                  ("phA", s_phA), ("phB", s_phB), ("cst", s_cst)):
        lcst = dma(d_[:], io[n][:])
    for ci, cv in enumerate(CONSTS):
        nc.const_aps.register(F32, cv, s_cst[:, ci:ci + 1])
    for eng in ("tensor", "scalar", "vector", "gpsimd"):
        pr.wait(eng, "L", lcst)

    if not dv["do_p0"]:
        pr.wait("sync", "S")
        if not dv["do_p1"]:
            return
    if dv["do_p0"]:
     with contextlib.ExitStack() as c0:
        e0 = c0.enter_context
        w_ld = [e0(nc.sbuf_tensor(f"w_ld{i}", [128, 4096], F32)) for i in range(2)]
        w_i32 = e0(nc.sbuf_tensor("w_i32", [128, 4096], I32))
        w_q = [e0(nc.sbuf_tensor(f"w_q{i}", [128, 4096], BF16)) for i in range(2)]
        sums = e0(nc.sbuf_tensor("sums", [128, 32], F32))
        rsum = e0(nc.sbuf_tensor("rsum", [128, 1], F32))
        msc = e0(nc.sbuf_tensor("msc", [1, 4], F32))
        ps_m = e0(nc.psum_tensor("ps_m", [128, 2], F32))

        for wi, (wf, wsc, cols, nchunk) in enumerate((
                (io["wgTf"], io["wgq_sc"], 4096, 8),
                (io["wuTf"], io["wuq_sc"], 4096, 8),
                (io["wdTf"], io["wdq_sc"], 1024, 32))):
            nelem = float(D * INNER)
            red_marks = {}
            for ci in range(nchunk):
                buf = w_ld[ci % 2]
                pr.wait("sync", "V", red_marks.get(ci - 2))
                lm = dma(buf[:, 0:cols], wf[ci * 128:(ci + 1) * 128, :])
                pr.wait("vector", "L", lm)
                red_marks[ci] = pr.op("vector", lambda e, b=buf, c=cols, ci=ci: e.reduce_sum(
                    out=sums[:, ci:ci + 1], in_=b[:, 0:c],
                    axis=mybir.AxisListType.X, apply_absolute_value=True), inc="V")
            bounce = pr.val("V")
            pr.wait("gpsimd", "V", bounce)
            gb = pr.op("gpsimd", lambda e: e.engine_nop(), inc="G")
            pr.wait("vector", "G", gb)
            vm = pr.op("vector", lambda e, n=nchunk: e.reduce_sum(
                out=rsum[:], in_=sums[:, 0:n], axis=mybir.AxisListType.X), inc="V")
            pr.wait("tensor", "V", vm)
            pm = pr.op("tensor", lambda e: e.matmul(
                out=ps_m[0:1, 0:1], lhsT=s_onesc[:], rhs=rsum[:],
                start=True, stop=True), inc="P")
            pr.wait("vector", "P", pm)
            vm = pr.op("vector", lambda e, n=nelem: e.tensor_scalar(
                out=msc[0:1, 0:1], in0=ps_m[0:1, 0:1], scalar1=1.0 / n,
                scalar2=1e-5, op0=AluOpType.mult, op1=AluOpType.max), inc="V")
            pr.wait("scalar", "V", vm)
            am = pr.op("scalar", lambda e: e.activation(
                out=msc[0:1, 1:2], in_=msc[0:1, 0:1], func=AF.Ln), inc="A")
            pr.wait("vector", "A", am)
            vm = pr.op("vector", lambda e: e.tensor_copy(
                out=msc[0:1, 2:3], in_=msc[0:1, 1:2]), inc="V")
            pr.wait("scalar", "V", vm)
            am = pr.op("scalar", lambda e: e.activation(
                out=msc[0:1, 3:4], in_=msc[0:1, 2:3], func=AF.Exp, scale=-1.0), inc="A")
            pr.wait("tensor", "A", am)
            pm = pr.op("tensor", lambda e: e.matmul(
                out=ps_m[:, 0:1], lhsT=s_ones[:], rhs=msc[0:1, 3:4],
                start=True, stop=True), inc="P")
            pr.wait("vector", "P", pm)
            pr.op("vector", lambda e, w=wi: e.tensor_copy(
                out=s_wm[:, 2 * w:2 * w + 1], in_=ps_m[:, 0:1]), inc="V")
            vm = pr.val("V")
            pr.wait("tensor", "V", vm)
            pm = pr.op("tensor", lambda e: e.matmul(
                out=ps_m[:, 1:2], lhsT=s_ones[:], rhs=msc[0:1, 0:1],
                start=True, stop=True), inc="P")
            pr.wait("vector", "P", pm)
            vm = pr.op("vector", lambda e, w=wi: e.tensor_copy(
                out=s_wm[:, 2 * w + 1:2 * w + 2], in_=ps_m[:, 1:2]), inc="V")

            if dv["taps"] and wi == 0:
                pr.wait("sync", "V")
                pr.wait("sync", "A")
                store(io["tap_p0"][:, 0:32], sums[:])
                store(io["tap_p0"][:, 32:33], rsum[:])
                store(io["tap_p0"][0:1, 33:37], msc[:])
                store(io["tap_p0"][:, 37:38], s_wm[:, 0:1])
            st_marks = {}
            q_marks = {}
            for ci in range(nchunk):
                buf = w_ld[ci % 2]
                qb = w_q[ci % 2]
                pr.wait("sync", "V", q_marks.get(ci - 2))
                lm = dma(buf[:, 0:cols], wf[ci * 128:(ci + 1) * 128, :])
                pr.wait("vector", "L", lm)
                pr.wait("vector", "S", st_marks.get(ci - 2))
                pr.op("vector", lambda e, b=buf, c=cols, w=wi: e.tensor_scalar(
                    out=b[:, 0:c], in0=b[:, 0:c], scalar1=s_wm[:, 2 * w:2 * w + 1],
                    scalar2=None, op0=AluOpType.mult), inc="V")
                pr.op("vector", lambda e, b=buf, c=cols: e.tensor_copy(
                    out=w_i32[:, 0:c], in_=b[:, 0:c]), inc="V")
                vm = pr.op("vector", lambda e, q=qb, c=cols: e.tensor_scalar(
                    out=q[:, 0:c], in0=w_i32[:, 0:c], scalar1=1.0, scalar2=-1.0,
                    op0=AluOpType.min, op1=AluOpType.max), inc="V")
                q_marks[ci] = vm
                pr.wait("sync", "V", vm)
                st_marks[ci] = store(wsc[ci * 128:(ci + 1) * 128, :], qb[:, 0:cols])

    for eng in ("sync", "tensor", "scalar", "vector", "gpsimd"):
        pr.wait(eng, "S")

    if not dv["do_p1"]:
        pr.wait("sync", "S")
        return

    for half in range(dv["halves"]):
        hoff = half * 512
        with contextlib.ExitStack() as c1:
            e1 = c1.enter_context
            QT = e1(nc.sbuf_tensor(f"QT{half}", [128, 4, ng * 512], BF16))
            KT = e1(nc.sbuf_tensor(f"KT{half}", [128, 4, nkv * 128], BF16))
            Vsb = e1(nc.sbuf_tensor(f"Vsb{half}", [128, nkv, 8, 65], BF16))
            outn = e1(nc.sbuf_tensor(f"outn{half}", [128, 4 * ng, 512], BF16))
            racc = e1(nc.sbuf_tensor(f"racc{half}", [128, 8], F32))
            nbuf = e1(nc.sbuf_tensor(f"nbuf{half}", [128, 4, 48], F32))
            bscr = e1(nc.sbuf_tensor(f"bscr{half}", [4, 8], F32))
            brow = e1(nc.sbuf_tensor(f"brow{half}", [1, 4], F32))

            with contextlib.ExitStack() as c2:
                e2 = c2.enter_context
                wkv = e2(nc.sbuf_tensor(f"wkv{half}", [128, 8, 1024], BF16))
                wqb = e2(nc.sbuf_tensor(f"wqb{half}", [128, 8, 512], BF16))
                wldf = e2(nc.sbuf_tensor(f"wldf{half}", [128, 512], F32))
                xtl = [e2(nc.sbuf_tensor(f"xtl{half}_{i}", [128, 8, 128], F32)) for i in range(2)]
                xtb = [e2(nc.sbuf_tensor(f"xtb{half}_{i}", [128, 8, 128], BF16)) for i in range(2)]
                rC = [e2(nc.sbuf_tensor(f"rC{half}_{i}", [128, 512], F32)) for i in range(2)]
                rS = [e2(nc.sbuf_tensor(f"rS{half}_{i}", [128, 512], F32)) for i in range(2)]
                ktok = e2(nc.sbuf_tensor(f"ktok{half}", [128, 512], F32))
                qtok = e2(nc.sbuf_tensor(f"qtok{half}", [128, 512], F32))
                m1 = e2(nc.sbuf_tensor(f"m1{half}", [128, 512], F32))
                m2 = e2(nc.sbuf_tensor(f"m2{half}", [128, 512], F32))
                krope = e2(nc.sbuf_tensor(f"krope{half}", [128, 512], BF16))
                qrope = e2(nc.sbuf_tensor(f"qrope{half}", [128, 512], BF16))
                sqs = e2(nc.sbuf_tensor(f"sqs{half}", [128, 512], F32))
                red4 = e2(nc.sbuf_tensor(f"red4{half}", [128, 4], F32))
                ps_k = [e2(nc.psum_tensor(f"ps_k{half}_{i}", [128, 512], F32)) for i in range(2)]
                ps_v = [e2(nc.psum_tensor(f"ps_v{half}_{i}", [128, 512], F32)) for i in range(2)]
                ps_q = [e2(nc.psum_tensor(f"ps_q{half}_{i}", [128, 512], F32)) for i in range(2)]
                ps_trb = e2(nc.psum_tensor(f"ps_trb{half}", [128, 512], BF16))
                ps_tr4 = e2(nc.psum_tensor(f"ps_tr4{half}", [128, 256], F32))

                wload_list = (((io["wkT"], 0), (io["wvT"], 1), (io["wqT"], 2))
                              if (half == 0 or dv["h1_wload"]) else ())
                for wi, (wsrc, wdst) in enumerate(wload_list):
                    for i in range(8):
                        pr.wait("sync", "V")
                        lm = dma(wldf[:], wsrc[i * 128:(i + 1) * 128, hoff:hoff + 512])
                        pr.wait("vector", "L", lm)
                        if wdst == 0:
                            pr.op("vector", lambda e, i=i: e.tensor_copy(
                                out=wkv[:, i, 0:512], in_=wldf[:]), inc="V")
                        elif wdst == 1:
                            pr.op("vector", lambda e, i=i: e.tensor_copy(
                                out=wkv[:, i, 512:1024], in_=wldf[:]), inc="V")
                        else:
                            pr.op("vector", lambda e, i=i: e.tensor_copy(
                                out=wqb[:, i, :], in_=wldf[:]), inc="V")

                gm_ones = pr.op("gpsimd", lambda e: e.memset(Vsb[:], 1.0), inc="G")
                pr.op("vector", lambda e: e.memset(racc[:], 0.0))
                pr.op("vector", lambda e: e.memset(nbuf[:], 0.0), inc="V")

                mk = {}
                nkv_h = nkv if half == 0 else min(nkv, dv["h1_kv"])
                for s in range(nkv_h):
                    bi = s % 2
                    is_q = (s % 8) >= 4
                    G = s // 8
                    qcol = G * 512 + ((s % 8) - 4) * 128

                    pr.wait("sync", "V", mk.get(("cast", s - 2)))
                    lx = dma(xtl[bi][:], io["xT"][:, s * 128:(s + 1) * 128]
                             .rearrange("(c p) t -> p c t", p=128))
                    pr.wait("sync", "G", mk.get(("rope", s - 2)))
                    dma(rC[bi][:], io["ropeC"][s * 128:(s + 1) * 128, :])
                    ls = dma(rS[bi][:], io["ropeS"][s * 128:(s + 1) * 128, :])
                    pr.wait("vector", "L", lx)
                    pr.wait("vector", "P", mk.get(("pm", s - 2)))
                    mk[("cast", s)] = pr.op("vector", lambda e, b=bi: e.tensor_copy(
                        out=xtb[b][:], in_=xtl[b][:]), inc="V")

                    pr.wait("tensor", "V", mk[("cast", s)])
                    pr.wait("tensor", "V", mk.get(("pevac", s - 2)))
                    pr.op("tensor", lambda e, ho=hoff, pa=bi: e.matmul(
                        out=ps_k[pa][:], lhsT=s_ones[:], rhs=s_bk[0:1, ho:ho + 512],
                        start=True, stop=False))
                    pr.op("tensor", lambda e, ho=hoff, pa=bi: e.matmul(
                        out=ps_v[pa][:], lhsT=s_ones[:], rhs=s_bv[0:1, ho:ho + 512],
                        start=True, stop=False))
                    for i in range(8):
                        pr.op("tensor", lambda e, b=bi, i=i, pa=bi: e.matmul(
                            out=ps_k[pa][:], lhsT=xtb[b][:, i, :], rhs=wkv[:, i, 0:512],
                            start=False, stop=(i == 7)))
                        pr.op("tensor", lambda e, b=bi, i=i, pa=bi: e.matmul(
                            out=ps_v[pa][:], lhsT=xtb[b][:, i, :], rhs=wkv[:, i, 512:1024],
                            start=False, stop=(i == 7)))
                    if is_q:
                        pr.op("tensor", lambda e, ho=hoff, pa=bi: e.matmul(
                            out=ps_q[pa][:], lhsT=s_ones[:], rhs=s_bq[0:1, ho:ho + 512],
                            start=True, stop=False))
                        for i in range(8):
                            pr.op("tensor", lambda e, b=bi, i=i, pa=bi: e.matmul(
                                out=ps_q[pa][:], lhsT=xtb[b][:, i, :], rhs=wqb[:, i, :],
                                start=False, stop=(i == 7)))
                    mk[("pm", s)] = pr.mark("tensor", "P")
                    if s > 0:
                        mk[("emit_tr",)](s - 1)

                    pr.wait("vector", "P", mk[("pm", s)])
                    if s == 0:
                        pr.wait("vector", "G", gm_ones)
                    pr.wait("vector", "G", mk.get(("rope", s - 1)))
                    pr.op("vector", lambda e, pa=bi: e.tensor_copy(out=ktok[:], in_=ps_k[pa][:]), inc="V")
                    pr.op("vector", lambda e, s=s, pa=bi: e.tensor_copy(
                        out=Vsb[:, s, :, 0:64],
                        in_=ps_v[pa].ap().rearrange("p (h d) -> p h d", d=64)), inc="V")
                    if is_q:
                        pr.op("vector", lambda e, pa=bi: e.tensor_copy(out=qtok[:], in_=ps_q[pa][:]), inc="V")
                    mk[("pevac", s)] = pr.val("V")

                    pr.wait("gpsimd", "V", mk[("pevac", s)])
                    pr.wait("gpsimd", "L", ls)
                    pr.wait("gpsimd", "P", mk.get(("tpm", s - 1)))
                    pr.wait("gpsimd", "A", mk.get(("sq", s - 1)))

                    def rope(tin, tout, b=bi):
                        pr.op("gpsimd", lambda e, t=tin, b=b: e.tensor_tensor(
                            out=m1[:], in0=t[:], in1=rC[b][:], op=AluOpType.mult))
                        pr.op("gpsimd", lambda e, t=tin, b=b: e.tensor_tensor(
                            out=m2[:, 0::2], in0=t[:, 1::2], in1=rS[b][:, 0::2], op=AluOpType.mult))
                        pr.op("gpsimd", lambda e, t=tin, b=b: e.tensor_tensor(
                            out=m2[:, 1::2], in0=t[:, 0::2], in1=rS[b][:, 1::2], op=AluOpType.mult))
                        return pr.op("gpsimd", lambda e, o=tout: e.tensor_tensor(
                            out=o[:], in0=m1[:], in1=m2[:], op=AluOpType.add), inc="G")

                    gk = rope(ktok, krope)
                    gq = rope(qtok, qrope) if is_q else gk
                    mk[("rope", s)] = gq

                    if not dv["do_norms"]:
                        mk[("sq", s)] = 0
                        mk[("red", s)] = 0
                    else:
                        pr.wait("scalar", "G", gk)
                        pr.wait("scalar", "V", mk.get(("red", s - 1)))
                        amk = pr.op("scalar", lambda e: e.activation(
                            out=sqs[:], in_=krope[:], func=AF.Square), inc="A")
                        pr.wait("vector", "A", amk)
                        vr = pr.op("vector", lambda e, s=s: e.tensor_reduce(
                            out=nbuf[:, :, s:s + 1],
                            in_=sqs.ap().rearrange("p (h d) -> p h d", d=128),
                            axis=mybir.AxisListType.X, op=AluOpType.add), inc="V")
                        if is_q:
                            qi = 32 + (s // 8) * 4 + (s % 8 - 4)
                            pr.wait("scalar", "V", vr)
                            pr.wait("scalar", "G", mk[("rope", s)])
                            amq = pr.op("scalar", lambda e: e.activation(
                                out=sqs[:], in_=qrope[:], func=AF.Square), inc="A")
                            pr.wait("vector", "A", amq)
                            vr = pr.op("vector", lambda e, qi=qi: e.tensor_reduce(
                                out=nbuf[:, :, qi:qi + 1],
                                in_=sqs.ap().rearrange("p (h d) -> p h d", d=128),
                                axis=mybir.AxisListType.X, op=AluOpType.add), inc="V")
                            mk[("sq", s)] = amq
                        else:
                            mk[("sq", s)] = amk
                        mk[("red", s)] = vr

                    def emit_tr(st):
                        isq = (st % 8) >= 4
                        qc = (st // 8) * 512 + ((st % 8) - 4) * 128
                        pr.wait("tensor", "G", mk[("rope", st)])
                        pr.wait("tensor", "V", mk.get(("tevac", st - 1)))
                        for j in range(4):
                            pr.op("tensor", lambda e, j=j: e.transpose(
                                out=ps_trb[:, j * 128:(j + 1) * 128],
                                in_=krope[:, j * 128:(j + 1) * 128], identity=s_idb[:]), inc="P")
                        pm = pr.val("P")
                        pr.wait("vector", "P", pm)
                        pr.op("vector", lambda e, st=st: e.tensor_copy(
                            out=KT[:, :, st * 128:(st + 1) * 128],
                            in_=ps_trb.ap().rearrange("p (h t) -> p h t", t=128)), inc="V")
                        tm = pr.val("V")
                        if isq:
                            pr.wait("tensor", "V", tm)
                            for j in range(4):
                                pr.op("tensor", lambda e, j=j: e.transpose(
                                    out=ps_trb[:, j * 128:(j + 1) * 128],
                                    in_=qrope[:, j * 128:(j + 1) * 128], identity=s_idb[:]), inc="P")
                            pm = pr.val("P")
                            pr.wait("vector", "P", pm)
                            pr.op("vector", lambda e, q=qc: e.tensor_copy(
                                out=QT[:, :, q:q + 128],
                                in_=ps_trb.ap().rearrange("p (h t) -> p h t", t=128)), inc="V")
                            tm = pr.val("V")
                        mk[("tevac", st)] = tm
                        mk[("tpm", st)] = pm
                    mk[("emit_tr",)] = emit_tr

                mk[("emit_tr",)](nkv_h - 1)

                if not dv["do_bounds"] or (half == 1 and not dv["h1_bounds"]):
                    continue
                vbounce()
                pr.op("vector", lambda e: e.tensor_reduce(
                    out=racc.ap().rearrange("p (a b) -> p a b", b=1)[:, 0:4, :],
                    in_=nbuf[:, :, 32:48], axis=mybir.AxisListType.X,
                    op=AluOpType.max))
                pr.op("vector", lambda e: e.tensor_reduce(
                    out=racc.ap().rearrange("p (a b) -> p a b", b=1)[:, 4:8, :],
                    in_=nbuf[:, :, 0:32], axis=mybir.AxisListType.X,
                    op=AluOpType.max), inc="V")
                vbounce()
                pr.op("vector", lambda e: e.tensor_scalar(
                    out=racc[:], in0=racc[:], scalar1=1e-30, scalar2=None,
                    op0=AluOpType.max), inc="V")
                pr.wait("tensor", "V")
                pr.op("tensor", lambda e: e.transpose(
                    out=ps_tr4[0:4, 0:128], in_=racc[:, 0:4], identity=s_idf[:]), inc="P")
                pm = pr.op("tensor", lambda e: e.transpose(
                    out=ps_tr4[0:4, 128:256], in_=racc[:, 4:8], identity=s_idf[:]), inc="P")
                pr.wait("vector", "P", pm)
                pr.op("vector", lambda e: e.tensor_reduce(
                    out=bscr[0:4, 0:1], in_=ps_tr4[0:4, 0:128],
                    axis=mybir.AxisListType.X, op=AluOpType.max), inc="V")
                vm = pr.op("vector", lambda e: e.tensor_reduce(
                    out=bscr[0:4, 1:2], in_=ps_tr4[0:4, 128:256],
                    axis=mybir.AxisListType.X, op=AluOpType.max), inc="V")
                pr.wait("scalar", "V", vm)
                am = pr.op("scalar", lambda e: e.activation(
                    out=bscr[0:4, 2:3], in_=bscr[0:4, 0:1], func=AF.Ln), inc="A")
                pr.wait("vector", "A", am)
                vm = pr.op("vector", lambda e: e.tensor_copy(
                    out=bscr[0:4, 4:5], in_=bscr[0:4, 2:3]), inc="V")
                pr.wait("scalar", "V", vm)
                am = pr.op("scalar", lambda e: e.activation(
                    out=bscr[0:4, 3:4], in_=bscr[0:4, 1:2], func=AF.Ln), inc="A")
                pr.wait("vector", "A", am)
                vm = pr.op("vector", lambda e: e.tensor_tensor(
                    out=bscr[0:4, 5:6], in0=bscr[0:4, 4:5], in1=bscr[0:4, 3:4],
                    op=AluOpType.add), inc="V")
                pr.wait("scalar", "V", vm)
                am = pr.op("scalar", lambda e: e.activation(
                    out=bscr[0:4, 6:7], in_=bscr[0:4, 5:6], func=AF.Exp,
                    scale=0.5, bias=-LN8), inc="A")
                pr.wait("vector", "A", am)
                vm = pr.op("vector", lambda e: e.tensor_scalar(
                    out=bscr[0:4, 7:8], in0=bscr[0:4, 6:7], scalar1=-1.0,
                    scalar2=None, op0=AluOpType.mult), inc="V")
                pr.wait("tensor", "V", vm)
                pm = pr.op("tensor", lambda e: e.transpose(
                    out=ps_tr4[0:1, 0:4], in_=bscr[0:4, 7:8],
                    identity=s_idf[0:4, 0:4]), inc="P")
                pr.wait("vector", "P", pm)
                vm = pr.op("vector", lambda e: e.tensor_copy(
                    out=brow[0:1, 0:4], in_=ps_tr4[0:1, 0:4]), inc="V")
                pr.wait("tensor", "V", vm)
                pm = pr.op("tensor", lambda e: e.matmul(
                    out=ps_tr4[:, 0:4], lhsT=s_ones[:], rhs=brow[0:1, 0:4],
                    start=True, stop=True), inc="P")
                pr.wait("vector", "P", pm)
                pr.op("vector", lambda e: e.tensor_copy(out=s_negM[:], in_=ps_tr4[:, 0:4]), inc="V")
                vm = pr.op("vector", lambda e: e.tensor_scalar(
                    out=s_negM2[:], in0=s_negM[:], scalar1=s_phA[:, 0:1],
                    scalar2=s_phB[:, 0:1], op0=AluOpType.mult, op1=AluOpType.add), inc="V")
                pr.wait("scalar", "V", vm)
                pr.wait("tensor", "V", vm)

                if dv["taps"] and half == 0:
                    pr.wait("sync", "V", vm)
                    store(io["tap_qt"][:], QT[:])
                    store(io["tap_kt"][:], KT[:])
                    store(io["tap_vs"][:], Vsb[:])
                    store(io["tap_nm"][:, 0:4], s_negM[:])
                    store(io["tap_nm"][:, 4:8], s_negM2[:])

            if not dv["do_p2"]:
                continue
            if dv["p2_half"] >= 0 and half != dv["p2_half"]:
                continue

            with contextlib.ExitStack() as c3:
                e3 = c3.enter_context
                ps_S = [e3(nc.psum_tensor(f"ps_S{half}_{i}", [128, 1024], F32)) for i in range(2)]
                ps_oA = e3(nc.psum_tensor(f"ps_oA{half}", [128, 512], F32))
                ps_oB = e3(nc.psum_tensor(f"ps_oB{half}", [128, 512], F32))
                ps_to = e3(nc.psum_tensor(f"ps_to{half}", [128, 260], F32))
                Pt = [e3(nc.sbuf_tensor(f"Pt{half}_{i}", [128, 1024], BF16)) for i in range(4)]
                o_raw = e3(nc.sbuf_tensor(f"o_raw{half}", [65, 2, 512], F32))
                tok = e3(nc.sbuf_tensor(f"tok{half}", [128, 2, 4, 65], F32))
                rcl = e3(nc.sbuf_tensor(f"rcl{half}", [128, 8], F32))
                rc = e3(nc.sbuf_tensor(f"rc{half}", [128, 8], F32))

                exp_mark = [0, 0]
                pv_mark = [0, 0, 0, 0]
                score_mark = [0, 0]
                pv_dep = [("A", 0)] * 4
                oraw_trans_mark = 0
                oraw_copy_prev = 0
                last_norm = 0
                for hp in range(4):
                    for G in range(ng):
                        E = 8 * (G + 1)

                        def emit_pv(j, hp=hp, G=G, E=E):
                            pb = j % 4
                            slj = (8 * G + 7 - j) if j < 4 else (j - 4)
                            dep_s, dep_v = pv_dep[pb]
                            pr.wait("tensor", dep_s, dep_v)
                            if j == 0:
                                pr.wait("tensor", "A", oraw_copy_prev2[0])
                            for h in range(2):
                                bank = ps_oA if h == 0 else ps_oB
                                hh = 2 * hp + h
                                pr.op("tensor", lambda e, b=bank, s=slj, hh=hh, pb=pb, h=h,
                                      st=(j == 0), sp=(j == E - 1): e.matmul(
                                    out=b[0:65, :], lhsT=Vsb[:, s, hh, 0:65],
                                    rhs=Pt[pb][:, h * 512:(h + 1) * 512], start=st, stop=sp))
                            pv_mark[pb] = pr.mark("tensor", "P")

                        oraw_copy_prev2 = [oraw_copy_prev]
                        for j in range(E):
                            pa = j % 2
                            slj = (8 * G + 7 - j) if j < 4 else (j - 4)
                            pr.wait("tensor", "A", exp_mark[pa])
                            for h in range(2):
                                pr.op("tensor", lambda e, h=h, s=slj, pa=pa, hp=hp, G=G: e.matmul(
                                    out=ps_S[pa][:, h * 512:(h + 1) * 512],
                                    lhsT=KT[64 * h:64 * h + 64, hp, s * 128:(s + 1) * 128],
                                    rhs=QT[64 * h:64 * h + 64, hp, G * 512:(G + 1) * 512],
                                    start=True, stop=True,
                                    tile_position=(64 * h, 0)))
                            score_mark[pa] = pr.mark("tensor", "P")
                            if j >= 2:
                                emit_pv(j - 2)
                            pb = j % 4
                            pr.wait("scalar", "P", score_mark[pa])
                            pr.wait("scalar", "P", pv_mark[pb])
                            if 4 <= j < 8:
                                am = pr.op("scalar", lambda e, hp=hp, pa=pa, pb=pb: e.activation(
                                    out=Pt[pb][:], in_=ps_S[pa][:], func=AF.Exp,
                                    bias=s_negM2[:, hp:hp + 1], scale=s_scph[:, 0:1]), inc="A")
                            else:
                                am = pr.op("scalar", lambda e, hp=hp, pa=pa, pb=pb: e.activation(
                                    out=Pt[pb][:], in_=ps_S[pa][:], func=AF.Exp,
                                    bias=s_negM[:, hp:hp + 1], scale=0.125), inc="A")
                            exp_mark[pa] = am
                            pv_dep[pb] = ("A", am)
                            if j < 4:
                                pr.wait("gpsimd", "A", am)
                                gm = pr.op("gpsimd", lambda e, pb=pb, base=-384 + 128 * j: e.affine_select(
                                    out=Pt[pb].ap().rearrange("p (x c) -> p x c", c=512),
                                    in_=Pt[pb].ap().rearrange("p (x c) -> p x c", c=512),
                                    pattern=[[0, 2], [1, 512]],
                                    compare_op=AluOpType.is_ge, fill=0.0,
                                    base=base, channel_multiplier=-1), inc="G")
                                pv_dep[pb] = ("G", gm)
                        if E >= 2:
                            emit_pv(E - 2)
                        emit_pv(E - 1)

                        pr.wait("scalar", "P", max(pv_mark))
                        pr.wait("scalar", "P", oraw_trans_mark)
                        pr.wait("scalar", "V", last_norm)
                        for h in range(2):
                            bank = ps_oA if h == 0 else ps_oB
                            pr.op("scalar", lambda e, b=bank, h=h: e.copy(
                                out=o_raw[0:65, h, :], in_=b[0:65, :]))
                        am = pr.mark("scalar", "A")
                        oraw_copy = am
                        oraw_copy_prev = am
                        ev_prev = 0
                        for h in range(2):
                            pr.wait("tensor", "A", oraw_copy)
                            pr.wait("tensor", "V", ev_prev)
                            for i in range(4):
                                pr.op("tensor", lambda e, h=h, i=i: e.transpose(
                                    out=ps_to[:, i * 65:(i + 1) * 65],
                                    in_=o_raw[0:65, h, i * 128:(i + 1) * 128],
                                    identity=s_idf[0:65, 0:65]), inc="P")
                            pm = pr.val("P")
                            if h == 1:
                                oraw_trans_mark = pm
                            pr.wait("vector", "P", pm)
                            ev_prev = pr.op("vector", lambda e, h=h: e.tensor_copy(
                                out=tok[:, h, :, :],
                                in_=ps_to.ap().rearrange("p (i c) -> p i c", c=65)), inc="V")
                        pr.wait("scalar", "V", ev_prev)
                        am = pr.op("scalar", lambda e: e.activation(
                            out=rcl[:], in_=tok[:, :, :, 64:65].rearrange("p h i c -> p (h i c)"),
                            func=AF.Ln), inc="A")
                        pr.wait("vector", "A", am)
                        vm = pr.op("vector", lambda e: e.tensor_copy(out=rc[:], in_=rcl[:]), inc="V")
                        pr.wait("scalar", "V", vm)
                        am = pr.op("scalar", lambda e: e.activation(
                            out=rc[:], in_=rc[:], func=AF.Exp, scale=-1.0), inc="A")
                        pr.wait("vector", "A", am)
                        for h in range(2):
                            for i in range(4):
                                hh = 2 * hp + h
                                pr.op("vector", lambda e, h=h, i=i, hh=hh, G=G: e.tensor_scalar(
                                    out=outn[:, 4 * G + i, hh * 64:(hh + 1) * 64],
                                    in0=tok[:, h, i, 0:64],
                                    scalar1=rc[:, 4 * h + i:4 * h + i + 1], scalar2=None,
                                    op0=AluOpType.mult), inc="V")
                        last_norm = pr.val("V")
                pr.wait("sync", "V", last_norm)
                store(io["attn_sc"].rearrange("(a p) c -> p a c", p=128)[:, :, hoff:hoff + 512],
                      outn[:])
                if dv["taps"]:
                    store(io["tap_sc2"].rearrange("(a p) c -> p a c", p=128)[:, :, hoff:hoff + 512],
                          outn[:])
                for eng in ("sync", "tensor", "scalar", "vector", "gpsimd"):
                    pr.wait(eng, "S")

    if not dv["do_p3"]:
        pr.wait("sync", "S")
        return

    pr.wait("sync", "S")
    with contextlib.ExitStack() as c4:
        e4 = c4.enter_context
        h1 = e4(nc.sbuf_tensor("h1", [128, 4, 1024], F32))
        h2 = h1
        xq = e4(nc.sbuf_tensor("xq", [128, 1024], BF16))
        xqT = e4(nc.sbuf_tensor("xqT", [128, 8, 512], BF16))
        guT = e4(nc.sbuf_tensor("guT", [128, 32, 512], BF16))
        fg = e4(nc.sbuf_tensor("fg", [128, 4], F32))
        fu = e4(nc.sbuf_tensor("fu", [128, 4], F32))
        fd = e4(nc.sbuf_tensor("fd", [128, 4], F32))
        lnt = e4(nc.sbuf_tensor("lnt", [128, 16], F32))
        stats = e4(nc.sbuf_tensor("stats", [128, 12], F32))
        mv = e4(nc.sbuf_tensor("mv", [128, 2], F32))
        h1n = e4(nc.sbuf_tensor("h1n", [128, 1024], F32))
        hi32 = e4(nc.sbuf_tensor("hi32", [128, 1024], I32))
        xht = e4(nc.sbuf_tensor("xht", [128, 1024], F32))
        outs = e4(nc.sbuf_tensor("outs", [128, 1024], F32))
        ps_y = e4(nc.psum_tensor("ps_y", [128, 1024], F32))
        ps_tb = e4(nc.psum_tensor("ps_tb", [128, 512], BF16))

        def transpose_1024(src_ap_fn, dst, dst_col, n=8):
            for g0 in range(0, n, 4):
                pr.wait("tensor", "V")
                for j in range(4):
                    pr.op("tensor", lambda e, g0=g0, j=j: e.transpose(
                        out=ps_tb[:, j * 128:(j + 1) * 128],
                        in_=src_ap_fn(g0 + j), identity=s_idb[:]), inc="P")
                pm = pr.val("P")
                pr.wait("vector", "P", pm)
                pr.op("vector", lambda e, g0=g0: e.tensor_copy(
                    out=dst[:, g0:g0 + 4, dst_col:dst_col + 128],
                    in_=ps_tb.ap().rearrange("p (a b) -> p a b", b=128)), inc="V")

        nstore = 0
        prev_store = 0
        for ck in range(4):
            cA = contextlib.ExitStack()
            aol = cA.enter_context(nc.sbuf_tensor(f"aol{ck}", [128, 1024], BF16))
            aoT = cA.enter_context(nc.sbuf_tensor(f"aoT{ck}", [128, 8, 512], BF16))
            woTb = cA.enter_context(nc.sbuf_tensor(f"woTb{ck}", [128, 8, 1024], BF16))
            with nc.sbuf_tensor(f"wldo{ck}", [128, 1024], F32) as wldo:
                for i in range(8):
                    pr.wait("sync", "V")
                    lm = dma(wldo[:], io["woT"][i * 128:(i + 1) * 128, :])
                    pr.wait("vector", "L", lm)
                    pr.op("vector", lambda e, i=i: e.tensor_copy(out=woTb[:, i, :], in_=wldo[:]), inc="V")
            for i in range(4):
                pr.wait("sync", "V")
                lm = dma(aol[:], io["attn_sc"][(4 * ck + i) * 128:(4 * ck + i + 1) * 128, :])
                pr.wait("tensor", "L", lm)
                transpose_1024(lambda j: aol[:, j * 128:(j + 1) * 128], aoT, i * 128)

            for i in range(4):
                pr.wait("sync", "V")
                lxh = dma(xht[:], io["xh"][(4 * ck + i) * 128:(4 * ck + i + 1) * 128, :])
                pr.wait("tensor", "V")
                pr.op("tensor", lambda e: e.matmul(
                    out=ps_y[:, 0:512], lhsT=s_ones[:], rhs=s_bo[0:1, 0:512],
                    start=True, stop=False))
                pr.op("tensor", lambda e: e.matmul(
                    out=ps_y[:, 512:1024], lhsT=s_ones[:], rhs=s_bo[0:1, 512:1024],
                    start=True, stop=False))
                for c in range(8):
                    pr.op("tensor", lambda e, c=c, i=i: e.matmul(
                        out=ps_y[:, 0:512], lhsT=aoT[:, c, i * 128:(i + 1) * 128],
                        rhs=woTb[:, c, 0:512], start=False, stop=(c == 7)))
                    pr.op("tensor", lambda e, c=c, i=i: e.matmul(
                        out=ps_y[:, 512:1024], lhsT=aoT[:, c, i * 128:(i + 1) * 128],
                        rhs=woTb[:, c, 512:1024], start=False, stop=(c == 7)))
                pm = pr.mark("tensor", "P")
                pr.wait("vector", "P", pm)
                pr.wait("vector", "L", lxh)
                vm_h1 = pr.op("vector", lambda e, i=i: e.scalar_tensor_tensor(
                    out=h1[:, i, :], in0=ps_y[:], scalar=0.0, in1=xht[:],
                    op0=AluOpType.add, op1=AluOpType.add), inc="V")
                if dv["taps"]:
                    pr.wait("sync", "V", vm_h1)
                    store(io["tap_h1"][(4 * ck + i) * 128:(4 * ck + i + 1) * 128, :],
                          h1[:, i, :])

                pr.op("vector", lambda e, i=i: e.bn_stats(out=stats[:, 0:6], in_=h1[:, i, 0:512]))
                pr.op("vector", lambda e, i=i: e.bn_stats(out=stats[:, 6:12], in_=h1[:, i, 512:1024]), inc="V")
                vbounce()
                vm = pr.op("vector", lambda e: e.bn_aggr(out=mv[:], in_=stats[:]), inc="V")
                pr.wait("scalar", "V", vm)
                am = pr.op("scalar", lambda e: e.activation(
                    out=lnt[:, 0:1], in_=mv[:, 1:2], func=AF.Ln, bias=1e-5), inc="A")
                pr.wait("vector", "A", am)
                vm = pr.op("vector", lambda e: e.tensor_copy(out=lnt[:, 1:2], in_=lnt[:, 0:1]), inc="V")
                pr.wait("scalar", "V", vm)
                am = pr.op("scalar", lambda e: e.activation(
                    out=lnt[:, 2:3], in_=lnt[:, 1:2], func=AF.Exp, scale=-0.5), inc="A")
                pr.wait("vector", "A", am)
                pr.op("vector", lambda e, i=i: e.tensor_scalar(
                    out=h1n[:], in0=h1[:, i, :], scalar1=mv[:, 0:1], scalar2=lnt[:, 2:3],
                    op0=AluOpType.subtract, op1=AluOpType.mult), inc="V")
                vq = pr.op("vector", lambda e: e.tensor_reduce(
                    out=lnt[:, 3:4], in_=h1n[:], axis=mybir.AxisListType.X,
                    op=AluOpType.max, apply_absolute_value=True), inc="V")
                pr.wait("gpsimd", "V", vq)
                gq_ = pr.op("gpsimd", lambda e: e.tensor_scalar(
                    out=lnt[:, 4:5], in0=lnt[:, 3:4], scalar1=1e-5, scalar2=None,
                    op0=AluOpType.max), inc="G")
                pr.wait("scalar", "G", gq_)
                pr.wait("vector", "G", gq_)
                am = pr.op("scalar", lambda e: e.activation(
                    out=lnt[:, 5:6], in_=lnt[:, 4:5], func=AF.Ln), inc="A")
                pr.wait("vector", "A", am)
                vm = pr.op("vector", lambda e: e.tensor_copy(out=lnt[:, 6:7], in_=lnt[:, 5:6]), inc="V")
                pr.wait("scalar", "V", vm)
                am = pr.op("scalar", lambda e: e.activation(
                    out=lnt[:, 7:8], in_=lnt[:, 6:7], func=AF.Exp, scale=-1.0,
                    bias=LN127), inc="A")
                pr.wait("vector", "A", am)
                pr.op("vector", lambda e: e.tensor_scalar(
                    out=h1n[:], in0=h1n[:], scalar1=lnt[:, 7:8], scalar2=None,
                    op0=AluOpType.mult), inc="V")
                pr.op("vector", lambda e: e.tensor_copy(out=hi32[:], in_=h1n[:]), inc="V")
                vm = pr.op("vector", lambda e: e.tensor_copy(
                    out=xq[:], in_=hi32[:, 0:1024]), inc="V")
                pr.op("vector", lambda e, i=i: e.tensor_scalar(
                    out=fg[:, i:i + 1], in0=lnt[:, 4:5], scalar1=1.0 / 127.0,
                    scalar2=s_wm[:, 1:2], op0=AluOpType.mult, op1=AluOpType.mult))
                pr.op("vector", lambda e, i=i: e.tensor_scalar(
                    out=fu[:, i:i + 1], in0=lnt[:, 4:5], scalar1=1.0 / 127.0,
                    scalar2=s_wm[:, 3:4], op0=AluOpType.mult, op1=AluOpType.mult), inc="V")
                transpose_1024(lambda j: xq[:, j * 128:(j + 1) * 128], xqT, i * 128)

            cA.close()
            with contextlib.ExitStack() as c5:
                e5 = c5.enter_context
                gu = e5(nc.sbuf_tensor(f"gu{ck}", [128, 4, 4096], F32))
                guq = e5(nc.sbuf_tensor(f"guq{ck}", [128, 4096], BF16))
                wgs = [e5(nc.sbuf_tensor(f"wgs{ck}_{i}", [128, 8, 512], BF16)) for i in range(2)]
                wus = [e5(nc.sbuf_tensor(f"wus{ck}_{i}", [128, 8, 512], BF16)) for i in range(2)]
                gsb = e5(nc.sbuf_tensor(f"gsb{ck}", [128, 512], F32))
                usb = e5(nc.sbuf_tensor(f"usb{ck}", [128, 512], F32))
                ps_g = [e5(nc.psum_tensor(f"ps_g{ck}_{i}", [128, 512], F32)) for i in range(2)]
                ps_u = [e5(nc.psum_tensor(f"ps_u{ck}_{i}", [128, 512], F32)) for i in range(2)]

                wl = {}
                evac_m = {}
                gu_m = {}
                pm_m = {}
                for o in range(8):
                    if o == 0:
                        wl[0] = (dma(wgs[0][:], io["wgq_sc"][:, 0:512]
                                     .rearrange("(c p) n -> p c n", p=128)),
                                 dma(wus[0][:], io["wuq_sc"][:, 0:512]
                                     .rearrange("(c p) n -> p c n", p=128)))
                    if o + 1 < 8:
                        pr.wait("sync", "P", pm_m.get((o - 1, 3), 0))
                        wl[o + 1] = (dma(wgs[(o + 1) % 2][:],
                                         io["wgq_sc"][:, (o + 1) * 512:(o + 2) * 512]
                                         .rearrange("(c p) n -> p c n", p=128)),
                                     dma(wus[(o + 1) % 2][:],
                                         io["wuq_sc"][:, (o + 1) * 512:(o + 2) * 512]
                                         .rearrange("(c p) n -> p c n", p=128)))
                    ob = o % 2
                    for i in range(4):
                        pa = i % 2
                        prev = (o, i - 2) if i >= 2 else (o - 1, i + 2)
                        pr.wait("tensor", "L", wl[o][1])
                        pr.wait("tensor", "A", evac_m.get(("g",) + prev, 0))
                        pr.wait("tensor", "V", evac_m.get(("u",) + prev, 0))
                        for c in range(8):
                            pr.op("tensor", lambda e, c=c, i=i, ob=ob, pa=pa: e.matmul(
                                out=ps_g[pa][:], lhsT=xqT[:, c, i * 128:(i + 1) * 128],
                                rhs=wgs[ob][:, c, :], start=(c == 0), stop=(c == 7)))
                            pr.op("tensor", lambda e, c=c, i=i, ob=ob, pa=pa: e.matmul(
                                out=ps_u[pa][:], lhsT=xqT[:, c, i * 128:(i + 1) * 128],
                                rhs=wus[ob][:, c, :], start=(c == 0), stop=(c == 7)))
                        pm = pr.mark("tensor", "P")
                        pm_m[(o, i)] = pm
                        pr.wait("scalar", "P", pm)
                        lastidx = (o, i - 1) if i >= 1 else (o - 1, 3)
                        pr.wait("scalar", "V", gu_m.get(lastidx, 0))
                        am = pr.op("scalar", lambda e, i=i, pa=pa: e.activation(
                            out=gsb[:], in_=ps_g[pa][:], func=AF.Silu,
                            scale=fg[:, i:i + 1]), inc="A")
                        evac_m[("g", o, i)] = am
                        pr.wait("vector", "P", pm)
                        vm_u = pr.op("vector", lambda e, i=i, pa=pa: e.tensor_scalar(
                            out=usb[:], in0=ps_u[pa][:], scalar1=fu[:, i:i + 1],
                            scalar2=None, op0=AluOpType.mult), inc="V")
                        evac_m[("u", o, i)] = vm_u
                        pr.wait("vector", "A", am)
                        gu_m[(o, i)] = pr.op("vector", lambda e, i=i, o=o: e.tensor_tensor(
                            out=gu[:, i, o * 512:(o + 1) * 512], in0=gsb[:], in1=usb[:],
                            op=AluOpType.mult), inc="V")

                if dv["taps"]:
                    pr.wait("sync", "V")
                    for i4 in range(4):
                        store(io["tap_gu"][(4 * ck + i4) * 128:(4 * ck + i4 + 1) * 128, :],
                              gu[:, i4, :])
                for i in range(4):
                    vq = pr.op("vector", lambda e, i=i: e.tensor_reduce(
                        out=lnt[:, 8:9], in_=gu[:, i, :], axis=mybir.AxisListType.X,
                        op=AluOpType.max, apply_absolute_value=True), inc="V")
                    pr.wait("gpsimd", "V", vq)
                    gq_ = pr.op("gpsimd", lambda e: e.tensor_scalar(
                        out=lnt[:, 9:10], in0=lnt[:, 8:9], scalar1=1e-5, scalar2=None,
                        op0=AluOpType.max), inc="G")
                    pr.wait("scalar", "G", gq_)
                    pr.wait("vector", "G", gq_)
                    am = pr.op("scalar", lambda e: e.activation(
                        out=lnt[:, 10:11], in_=lnt[:, 9:10], func=AF.Ln), inc="A")
                    pr.wait("vector", "A", am)
                    vm = pr.op("vector", lambda e: e.tensor_copy(
                        out=lnt[:, 11:12], in_=lnt[:, 10:11]), inc="V")
                    pr.wait("scalar", "V", vm)
                    am = pr.op("scalar", lambda e: e.activation(
                        out=lnt[:, 12:13], in_=lnt[:, 11:12], func=AF.Exp, scale=-1.0,
                        bias=LN127), inc="A")
                    pr.wait("vector", "A", am)
                    pr.op("vector", lambda e, i=i: e.tensor_scalar(
                        out=gu[:, i, :], in0=gu[:, i, :], scalar1=lnt[:, 12:13],
                        scalar2=None, op0=AluOpType.mult), inc="V")
                    for q4 in range(4):
                        pr.op("vector", lambda e, i=i, q4=q4: e.tensor_copy(
                            out=hi32[:], in_=gu[:, i, q4 * 1024:(q4 + 1) * 1024]), inc="V")
                        vm = pr.op("vector", lambda e, q4=q4: e.tensor_copy(
                            out=guq[:, q4 * 1024:(q4 + 1) * 1024], in_=hi32[:]), inc="V")
                    pr.op("vector", lambda e, i=i: e.tensor_scalar(
                        out=fd[:, i:i + 1], in0=lnt[:, 9:10], scalar1=1.0 / 127.0,
                        scalar2=s_wm[:, 5:6], op0=AluOpType.mult, op1=AluOpType.mult), inc="V")
                    for g0 in range(0, 32, 4):
                        pr.wait("tensor", "V")
                        for j in range(4):
                            pr.op("tensor", lambda e, g0=g0, j=j: e.transpose(
                                out=ps_tb[:, j * 128:(j + 1) * 128],
                                in_=guq[:, (g0 + j) * 128:(g0 + j + 1) * 128],
                                identity=s_idb[:]), inc="P")
                        pm = pr.val("P")
                        pr.wait("vector", "P", pm)
                        pr.op("vector", lambda e, g0=g0, i=i: e.tensor_copy(
                            out=guT[:, g0:g0 + 4, i * 128:(i + 1) * 128],
                            in_=ps_tb.ap().rearrange("p (a b) -> p a b", b=128)), inc="V")

            with contextlib.ExitStack() as c6:
                e6 = c6.enter_context
                wds = e6(nc.sbuf_tensor(f"wds{ck}", [128, 32, 512], BF16))
                ps_d = e6(nc.psum_tensor(f"ps_d{ck}", [128, 512], F32))

                for o2 in range(2):
                    pr.wait("sync", "P")
                    lw = dma(wds[:], io["wdq_sc"][:, o2 * 512:(o2 + 1) * 512]
                             .rearrange("(c p) n -> p c n", p=128))
                    for i in range(4):
                        pr.wait("tensor", "L", lw)
                        pr.wait("tensor", "V")
                        for c in range(32):
                            pr.op("tensor", lambda e, c=c, i=i: e.matmul(
                                out=ps_d[:], lhsT=guT[:, c, i * 128:(i + 1) * 128],
                                rhs=wds[:, c, :], start=(c == 0), stop=(c == 31)))
                        pm = pr.mark("tensor", "P")
                        pr.wait("vector", "P", pm)
                        pr.op("vector", lambda e, i=i, o2=o2: e.scalar_tensor_tensor(
                            out=h2[:, i, o2 * 512:(o2 + 1) * 512], in0=ps_d[:],
                            scalar=fd[:, i:i + 1], in1=h1[:, i, o2 * 512:(o2 + 1) * 512],
                            op0=AluOpType.mult, op1=AluOpType.add), inc="V")

            for i in range(4):
                pr.op("vector", lambda e, i=i: e.bn_stats(out=stats[:, 0:6], in_=h2[:, i, 0:512]))
                pr.op("vector", lambda e, i=i: e.bn_stats(out=stats[:, 6:12], in_=h2[:, i, 512:1024]), inc="V")
                vbounce()
                vm = pr.op("vector", lambda e: e.bn_aggr(out=mv[:], in_=stats[:]), inc="V")
                pr.wait("scalar", "V", vm)
                am = pr.op("scalar", lambda e: e.activation(
                    out=lnt[:, 0:1], in_=mv[:, 1:2], func=AF.Ln, bias=1e-5), inc="A")
                pr.wait("vector", "A", am)
                vm = pr.op("vector", lambda e: e.tensor_copy(out=lnt[:, 1:2], in_=lnt[:, 0:1]), inc="V")
                pr.wait("scalar", "V", vm)
                am = pr.op("scalar", lambda e: e.activation(
                    out=lnt[:, 2:3], in_=lnt[:, 1:2], func=AF.Exp, scale=-0.5), inc="A")
                pr.wait("vector", "A", am)
                if prev_store:
                    pr.wait("vector", "S", prev_store)
                vm = pr.op("vector", lambda e, i=i: e.tensor_scalar(
                    out=outs[:], in0=h2[:, i, :], scalar1=mv[:, 0:1], scalar2=lnt[:, 2:3],
                    op0=AluOpType.subtract, op1=AluOpType.mult), inc="V")
                pr.wait("sync", "V", vm)
                prev_store = store(io["y"][(4 * ck + i) * 128:(4 * ck + i + 1) * 128, :], outs[:])

    pr.wait("sync", "S")


def _rope_tables(pos):
    j = np.arange(32, dtype=np.float64)
    theta = 10000.0 ** (-j / 32.0)
    ang = pos.astype(np.float64)[:, None] * theta[None, :]
    c = np.cos(ang)
    s = np.sin(ang)
    C64 = np.empty((pos.shape[0], 64), np.float64)
    S64 = np.empty((pos.shape[0], 64), np.float64)
    C64[:, 0::2] = c
    C64[:, 1::2] = c
    S64[:, 0::2] = -s
    S64[:, 1::2] = s
    C = np.tile(C64, (1, 8)).astype(np.float32)
    S = np.tile(S64, (1, 8)).astype(np.float32)
    return np.ascontiguousarray(C), np.ascontiguousarray(S)


def _host_inputs(inputs, core):
    b, r = core // 2, core % 2
    x = np.asarray(inputs["L0_attn_x"], np.float32)[b]
    kvmap = kv_slot_map(r)
    x_kv = np.zeros((NKV * 128, D), np.float32)
    pos = np.zeros(NKV * 128, np.float32)
    for s, m in enumerate(kvmap):
        if m >= 0:
            x_kv[s * 128:(s + 1) * 128] = x[m * 128:(m + 1) * 128]
            pos[s * 128:(s + 1) * 128] = np.arange(m * 128, (m + 1) * 128)
    xT = np.ascontiguousarray(x_kv.T)
    ropeC, ropeS = _rope_tables(pos)
    qt = q_rows_tiles(r)
    xh = np.concatenate([x[m * 128:(m + 1) * 128] for m in qt], 0)
    f32 = lambda a: np.ascontiguousarray(np.asarray(a, np.float32))
    d = {
        "xT": xT, "ropeC": ropeC, "ropeS": ropeS,
        "wqT": f32(np.asarray(inputs["L0_attn_W_q"]).T),
        "wkT": f32(np.asarray(inputs["L0_attn_W_k"]).T),
        "wvT": f32(np.asarray(inputs["L0_attn_W_v"]).T),
        "woT": f32(np.asarray(inputs["L0_attn_W_o"]).T),
        "wgTf": f32(np.asarray(inputs["L0_ffn_w_gate"]).T),
        "wuTf": f32(np.asarray(inputs["L0_ffn_w_up"]).T),
        "wdTf": f32(np.asarray(inputs["L0_ffn_w_down"]).T),
        "bqr": f32(np.asarray(inputs["L0_attn_b_q"]).reshape(1, D)),
        "bkr": f32(np.asarray(inputs["L0_attn_b_k"]).reshape(1, D)),
        "bvr": f32(np.asarray(inputs["L0_attn_b_v"]).reshape(1, D)),
        "bor": f32(np.asarray(inputs["L0_attn_b_o"]).reshape(1, D)),
        "xh": np.ascontiguousarray(xh),
        "identf": np.eye(128, dtype=np.float32),
        "identb": np.eye(128, dtype=np.float32).astype(bf16),
        "onesrow": np.ones((1, 128), np.float32),
        "onescol": np.ones((128, 1), np.float32),
        "scph": np.full((128, 1), 0.125 if r == 0 else 0.0, np.float32),
        "phA": np.full((128, 1), 1.0 if r == 0 else 0.0, np.float32),
        "phB": np.full((128, 1), 0.0 if r == 0 else -40.0, np.float32),
        "cst": np.tile(np.asarray(CONSTS, np.float32), (128, 1)),
    }
    return d


def kernel(**inputs):
    nc = build_program()
    in_maps = [_host_inputs(inputs, c) for c in range(8)]
    res = run_bass_kernel_spmd(nc, in_maps, core_ids=list(range(8)))
    out = np.zeros((B, T, D), np.float32)
    for c in range(8):
        b, r = c // 2, c % 2
        yc = np.asarray(res.results[c]["y"], np.float32)
        for idx, m in enumerate(q_rows_tiles(r)):
            out[b, m * 128:(m + 1) * 128] = yc[idx * 128:(idx + 1) * 128]
    return out
